# version 42
# baseline (speedup 1.0000x reference)
import numpy as np
import ml_dtypes
from contextlib import ExitStack

import concourse.bass as bass
import concourse.mybir as mybir
import concourse.bass_utils as _bu
from concourse.bass_utils import run_bass_kernel_spmd

_orig_run_command = _bu.run_command


def _patched_run_command(argv, **kwargs):
    argv = ["--enable-ldw-opt=true" if a == "--enable-ldw-opt=false" else a
            for a in argv]
    return _orig_run_command(argv, **kwargs)


if _bu.run_command is not _patched_run_command:
    _bu.run_command = _patched_run_command

F32 = mybir.dt.float32
BF16 = mybir.dt.bfloat16
AF = mybir.ActivationFunctionType
ALU = mybir.AluOpType

N_CORE = 8
NB = 4
SEQ = 2048
D = 1024
NQ = 4
NH = 16
EPS = 1e-5

ENG_OF = {"sp_dma": "sp", "pl_dma": "pl", "act_dma": "act"}


def _const_inputs():
    p = np.arange(128)[:, None]
    w = np.arange(144)[None, :]
    return {
        "ident_f": np.eye(128, dtype=np.float32),
        "ident_b": np.eye(128, dtype=ml_dtypes.bfloat16),
        "ones_row": np.ones((1, 128), dtype=np.float32),
        "ones_col": np.ones((128, 1), dtype=np.float32),
        "mask0": (np.abs(p - w) <= 8).astype(ml_dtypes.bfloat16),
        "maskM": ((w >= p) & (w <= p + 16)).astype(ml_dtypes.bfloat16),
    }


class S:
    pass


class Ctx:

    def __init__(self, st, cur, eng, dry=False):
        self.S = st
        self.cur = cur
        self.e = eng
        self.dry = dry
        self.n = {k: 0 for k in ("pe", "dve", "act", "sp", "pl", "sp_dma", "pl_dma", "act_dma")}
        self.pend = {k: 0 for k in self.n}
        self.waited = {}
        self.bank_w = {}
        self.bank_r = {}

    def _wait(self, key, cnt):
        if ENG_OF.get(key, key) == self.cur:
            return
        val = cnt * (16 if key.endswith("_dma") else 1)
        if val <= 0 or self.waited.get(key, 0) >= val:
            return
        self.waited[key] = val
        if self.dry:
            self.S.waitset[key].add(val)
        else:
            self.e.wait_ge(self.S.sems[key], val)

    def raw_wait(self, key, cnt=None):
        c = self.n[key] if cnt is None else cnt
        val = c * (16 if key.endswith("_dma") else 1)
        if val <= 0 or self.waited.get(key, 0) >= val:
            return
        self.waited[key] = val
        if self.dry:
            self.S.waitset[key].add(val)
        else:
            self.e.wait_ge(self.S.sems[key], val)

    def q_serialize(self, qkey):
        if ENG_OF[qkey] == self.cur:
            self.raw_wait(qkey, self.n[qkey])

    def op(self, key, fn, deps=()):
        eng = ENG_OF.get(key, key)
        if eng == self.cur:
            for (tk, tc) in deps:
                cnt = self.n[tk] if tc is None else tc
                if ENG_OF.get(tk, tk) == self.cur:
                    self.raw_wait(tk, cnt)
                else:
                    self._wait(tk, cnt)
        self.n[key] += 1
        if eng == self.cur and not self.dry:
            if key.endswith("_dma"):
                fn(self.e).then_inc(self.S.sems[key], 16)
            else:
                self.pend[key] += 1
                if self.n[key] in self.S.waitset[key]:
                    fn(self.e).then_inc(self.S.sems[key], self.pend[key])
                    self.pend[key] = 0
                else:
                    fn(self.e)
        return self.n[key]

    def mm(self, bank, fn, deps=()):
        d = list(deps)
        rs = self.bank_r.get(bank)
        if rs:
            d += list(rs.items())
        n = self.op("pe", fn, d)
        self.bank_r[bank] = {}
        self.bank_w[bank] = n
        return n

    def rd(self, key, bank, fn, deps=()):
        d = list(deps) + [("pe", self.bank_w.get(bank, 0))]
        n = self.op(key, fn, d)
        self.bank_r.setdefault(bank, {})[key] = n
        return n


def wv(t, chunk, width, sub=None):
    lo = chunk * width + (sub[0] if sub else 0)
    hi = chunk * width + (sub[1] if sub else width)
    return t[:, lo:hi]


def build_state(nc, n_b):
    st = S()
    st.nc = nc
    st.n_b = n_b

    def din(name, shape, dtype=F32):
        return nc.dram_tensor(name, list(shape), dtype, kind="ExternalInput").ap()

    st.x = din("x", [n_b, SEQ, D])
    st.wqkv = din("wqkv", [D, 3 * D])
    st.wproj = din("wproj", [D, D])
    st.bproj = din("bproj", [D])
    st.cn_w1 = din("cn_w1", [D, D]); st.cn_b1 = din("cn_b1", [D])
    st.cn_w2 = din("cn_w2", [D, 512]); st.cn_b2 = din("cn_b2", [512])
    st.cn_w3 = din("cn_w3", [512, 256]); st.cn_b3 = din("cn_b3", [256])
    st.cn_w4 = din("cn_w4", [256, 1]); st.cn_b4 = din("cn_b4", [1])
    st.gc_w1 = din("gc_w1", [D, 512]); st.gc_b1 = din("gc_b1", [512])
    st.gc_w2 = din("gc_w2", [512, 256]); st.gc_b2 = din("gc_b2", [256])
    st.gc_w3 = din("gc_w3", [256, 1]); st.gc_b3 = din("gc_b3", [1])
    st.ar = din("ar", [1])
    st.ident_f = din("ident_f", [128, 128])
    st.ident_b = din("ident_b", [128, 128], BF16)
    st.ones_row = din("ones_row", [1, 128])
    st.ones_col = din("ones_col", [128, 1])
    st.mask0 = din("mask0", [128, 144], BF16)
    st.maskM = din("maskM", [128, 144], BF16)

    st.out = nc.dram_tensor("out", [n_b, 8, 128, SEQ], F32,
                        kind="ExternalOutput").ap()
    st.x1sp = nc.dram_tensor("x1sp", [n_b, 8, 128, SEQ], F32, kind="Internal").ap()

    st.ex = ExitStack()
    e = st.ex.enter_context

    def sb(name, shape, dtype):
        return e(nc.sbuf_tensor(name, list(shape), dtype))

    st.wqkv_sb = sb("wqkv_sb", [128, 8 * 3072], BF16)
    st.wproj_sb = sb("wproj_sb", [128, 8 * 1024], BF16)
    st.wcn1_sb = sb("wcn1_sb", [128, 8 * 1024], BF16)
    st.wcn2_sb = sb("wcn2_sb", [128, 8 * 512], BF16)
    st.wcn3_sb = sb("wcn3_sb", [128, 4 * 256], BF16)
    st.wcn4_sb = sb("wcn4_sb", [128, 2], BF16)
    st.wgc1_sb = sb("wgc1_sb", [128, 8 * 512], BF16)
    st.wgc2_sb = sb("wgc2_sb", [128, 4 * 256], BF16)
    st.wgc3_sb = sb("wgc3_sb", [128, 2], BF16)
    st.bproj_sb = sb("bproj_sb", [128, 8], F32)
    st.bcn1_sb = sb("bcn1_sb", [128, 8], F32)
    st.bcn2_sb = sb("bcn2_sb", [128, 4], F32)
    st.bcn3_sb = sb("bcn3_sb", [128, 2], F32)
    st.bgc1_sb = sb("bgc1_sb", [128, 4], F32)
    st.bgc2_sb = sb("bgc2_sb", [128, 2], F32)
    st.srow = sb("srow", [1, 64], F32)
    st.identf_sb = sb("identf_sb", [128, 128], F32)
    st.identb_sb = sb("identb_sb", [128, 128], BF16)
    st.onesrow_sb = sb("onesrow_sb", [1, 128], F32)
    st.onescol_sb = sb("onescol_sb", [128, 1], F32)
    st.mask0_sb = sb("mask0_sb", [128, 144], BF16)
    st.maskM_sb = sb("maskM_sb", [128, 144], BF16)

    st.xtok = sb("xtok", [128, 6 * 1024], F32)
    st.xtb = sb("xtb", [128, 8 * 528], BF16)
    st.qkvb = sb("qkvb", [128, 24 * 528], BF16)
    st.vwin = sb("vwin", [128, 4 * 256], BF16)
    st.ptsb = sb("ptsb", [128, 6 * 256], BF16)
    st.sacc = sb("sacc", [128, 8], F32)
    st.otsb = sb("otsb", [128, 8 * 512], BF16)
    st.f2 = sb("f2", [128, 8 * 512], F32)
    st.f1 = sb("f1", [128, 8 * 512], F32)
    st.gcparts = sb("gcparts", [128, 8 * 16], F32)
    st.gcsum = sb("gcsum", [128, 32], F32)
    st.gcbf = sb("gcbf", [128, 32], BF16)
    st.g1bf = sb("g1bf", [128, 16], BF16)
    st.g2bf = sb("g2bf", [128, 8], BF16)

    st.P = [e(nc.psum_tensor(f"P{i}", [128, 512], F32)) for i in range(8)]
    st.wcn1v = [wv(st.wcn1_sb, k, 1024) for k in range(8)]
    st.wcn2v = [wv(st.wcn2_sb, k, 512) for k in range(8)]
    st.wcn3v = [wv(st.wcn3_sb, k, 256) for k in range(4)]
    st.x1roll = st.f1[:, 1024:2048]
    st.tsc_a = st.f1[:, 2048:2560]
    st.rows_a = st.f1[0:1, 2560:3584]
    st.sq_a = [st.f1[:, 3584:4096], st.f1[:, 2048:2560]]
    xtbf32 = st.xtb[:].bitcast(F32)
    st.tsc_c = xtbf32[:, 0:512]
    st.sq_cB = [xtbf32[:, 512:1024], xtbf32[:, 1024:1536]]
    otsbf32 = st.otsb[:].bitcast(F32)
    st.rows_c = otsbf32[0:1, 1024:2048]
    st.sq_cA = [otsbf32[:, 0:512], otsbf32[:, 512:1024]]
    st.pb = [st.xtb[:, 1024 + 160 * i: 1024 + 160 * (i + 1)] for i in range(8)]
    st.pmb = [st.xtb[:, 2304 + 160 * i: 2304 + 160 * (i + 1)] for i in range(8)]
    wqf = st.wqkv_sb[:].bitcast(F32)
    st.sb0 = wqf[:, 0:4096]
    st.sb1 = wqf[:, 4096:8192]
    st.x1B = wqf[:, 8192:12288]
    st.obvB = st.wproj_sb[:, 0:4096]
    wpf = st.wproj_sb[:].bitcast(F32)
    st.tscB = wpf[:, 2048:2560]
    st.rowsB = wpf[0:1, 2560:3584]
    st.Pb = [st.P[i][:].bitcast(BF16) for i in range(8)]

    st.sems = {}
    for k in ("pe", "dve", "act", "sp", "pl", "sp_dma", "pl_dma", "act_dma"):
        st.sems[k] = e(nc.semaphore(f"sem_{k}"))
    st.waitset = {k: set() for k in st.sems}
    st.block = e(nc.Block())
    return st


SL_A2, SL_E7, SL_HB4, SL_HB3, SL_THG, SL_EPS, SL_CB2 = 0, 1, 2, 3, 4, 5, 8


def program(ctx):
    st = ctx.S
    setup(ctx)
    for b in range(st.n_b):
        for q in range(NQ):
            phase_a(ctx, b, q)
    if st.pending_ln is not None:
        pb_, pq_, prse = st.pending_ln
        st.pending_ln = None
        ln_finish_a(ctx, pb_, pq_, prse)
    vwA = c_views(st, 0)
    vwB = c_views(st, 1)
    emit_c_reload(ctx, 0, 0, vwA)
    emit_c_reload(ctx, 0, 1, vwB)
    st.c_obv_pre = {(0, 0): emit_c_obv_load(ctx, 0, 0, vwA),
                    (0, 1): emit_c_obv_load(ctx, 0, 1, vwB)}
    phase_b(ctx)
    for b in range(st.n_b):
        for qa in range(0, NQ, 2):
            phase_c_pair(ctx, b, qa)
    if ctx.cur == "sp":
        ctx.raw_wait("sp_dma")
    if ctx.cur == "pl":
        ctx.raw_wait("pl_dma")
    if ctx.cur == "act":
        ctx.raw_wait("act_dma")


def wdma(ctx, qkey, dst, src):
    kc = src.shape[0] // 128
    slow = src.shape[1] < 16
    ctx.op(qkey, lambda e: e.dma_start(
        out=dst.rearrange("p (kc m) -> p kc m", kc=kc),
        in_=src.rearrange("(kc p) m -> p kc m", p=128),
        allow_slow_non_contiguous=slow))


def setup(ctx):
    st = ctx.S
    st.loaded = set()
    st.load_cnt = {}
    st.roll_user = {}
    st.spill_last = {}
    st.pb_reader = {}
    st.c_rld = {}
    st.out_dma_last = {}
    st.c_pre = {}
    st.out_tr = [None, None]
    st.spill_q = {}
    st.c_obv_pre = {}
    st.pending_ln = None

    def cdma(fn):
        ctx.op("sp_dma", fn)
    cdma(lambda e: e.dma_start(out=st.identf_sb[:], in_=st.ident_f))
    cdma(lambda e: e.dma_start(out=st.identb_sb[:], in_=st.ident_b))
    cdma(lambda e: e.dma_start(out=st.onesrow_sb[:], in_=st.ones_row))
    cdma(lambda e: e.dma_start(out=st.onescol_sb[:], in_=st.ones_col))
    cdma(lambda e: e.dma_start(out=st.mask0_sb[:], in_=st.mask0))
    cdma(lambda e: e.dma_start(out=st.maskM_sb[:], in_=st.maskM))
    cdma(lambda e: e.dma_start(out=st.srow[0:1, 16:17], in_=st.ar.unsqueeze(0)))
    cdma(lambda e: e.dma_start(out=st.srow[0:1, 17:18], in_=st.cn_b4.unsqueeze(0)))
    cdma(lambda e: e.dma_start(out=st.srow[0:1, 18:19], in_=st.gc_b3.unsqueeze(0)))
    st.c_sp_consts = ctx.n["sp_dma"]
    ctx.q_serialize("sp_dma")
    for tc in tc_needed(0):
        cdma(lambda e, tc=tc: e.dma_start(
            out=wv(st.xtok, tc % 6, 1024),
            in_=st.x[0, 128 * tc:128 * tc + 128, :]))
        st.loaded.add((0, tc))
    for tc in tc_needed(0):
        st.load_cnt[(0, tc)] = ("sp_dma", ctx.n["sp_dma"])
    ctx.q_serialize("sp_dma")
    for dst, src in ((st.bproj_sb, st.bproj), (st.bcn1_sb, st.cn_b1),
                     (st.bcn2_sb, st.cn_b2), (st.bcn3_sb, st.cn_b3),
                     (st.bgc1_sb, st.gc_b1), (st.bgc2_sb, st.gc_b2)):
        cdma(lambda e, dst=dst, src=src: e.dma_start(
            out=dst[:], in_=src.rearrange("(c p) -> p c", p=128),
            allow_slow_non_contiguous=True))
    st.c_sp_bias = ctx.n["sp_dma"]
    wdma(ctx, "pl_dma", st.wqkv_sb[:], st.wqkv)
    st.c_pl_wqkv = ctx.n["pl_dma"]
    ctx.q_serialize("pl_dma")
    wdma(ctx, "pl_dma", st.wproj_sb[:], st.wproj)
    st.c_pl_wproj = ctx.n["pl_dma"]
    ctx.q_serialize("pl_dma")

    dep = [("sp_dma", st.c_sp_consts)]
    r = st.srow
    ctx.op("dve", lambda e: e.tensor_scalar(
        r[0:1, SL_A2:SL_A2 + 1], r[0:1, 16:17], 0.3, None, ALU.mult), dep)
    ctx.op("dve", lambda e: e.tensor_scalar(
        r[0:1, SL_E7:SL_E7 + 1], r[0:1, 16:17], 0.7, None, ALU.mult))
    ctx.op("dve", lambda e: e.tensor_scalar(
        r[0:1, SL_HB4:SL_HB4 + 1], r[0:1, 17:18], 0.5, None, ALU.mult))
    ctx.op("dve", lambda e: e.tensor_scalar(
        r[0:1, SL_HB3:SL_HB3 + 1], r[0:1, 18:19], 0.5, None, ALU.mult))

    ctx.op("dve", lambda e: e.memset(r[0:1, SL_EPS:SL_EPS + 1], EPS))
    for i in (6, 7):
        ctx.op("dve", lambda e, i=i: e.memset(st.P[i][:, 192:256], 0.0))
    if ctx.cur in ("pe", "dve", "act"):
        ctx.raw_wait("sp_dma", st.c_sp_consts)
    if ctx.cur == "act":
        ctx._wait("dve", ctx.n["dve"])


def tc_needed(q):
    lo = max(0, 4 * q - 1)
    hi = min(16, 4 * q + 5)
    return list(range(lo, hi))


def emit_x_loads(ctx, b, q, tcs, split=False):
    st = ctx.S
    todo = [tc for tc in tcs if (b, tc) not in st.loaded]
    if not todo:
        return
    qs = {tc: ("sp_dma" if split and i % 2 else "pl_dma")
          for i, tc in enumerate(todo)}
    for qk in set(qs.values()):
        ctx.q_serialize(qk)
    for tc in todo:
        ctx.op(qs[tc], lambda e, tc=tc: e.dma_start(
            out=wv(st.xtok, tc % 6, 1024),
            in_=st.x[b, 128 * tc:128 * tc + 128, :]),
            deps=[("pe", None)])
        st.loaded.add((b, tc))
    for tc in todo:
        st.load_cnt[(b, tc)] = (qs[tc], ctx.n[qs[tc]])


def phase_a(ctx, b, q):
    st = ctx.S
    base = 512 * q - 8

    for tc in sorted(tc_needed(q), key=lambda tc: st.load_cnt[(b, tc)][1]):
        col0 = 128 * tc - base
        slot = tc % 6
        for f in range(8):
            bank = (f + 4 * tc) % 4
            deps = [st.load_cnt[(b, tc)]] if f == 0 else []
            ctx.mm(bank, lambda e, slot=slot, f=f, bank=bank: e.transpose(
                st.P[bank][:, 0:128],
                wv(st.xtok, slot, 1024, (128 * f, 128 * f + 128)),
                st.identf_sb[:]), deps)
            if col0 == -120:
                src = st.P[bank][:, 120:128]
                dst = wv(st.xtb, f, 528, (0, 8))
            elif col0 == 520:
                src = st.P[bank][:, 0:8]
                dst = wv(st.xtb, f, 528, (520, 528))
            else:
                src = st.P[bank][:, 0:128]
                dst = wv(st.xtb, f, 528, (col0, col0 + 128))
            ctx.rd("act", bank, lambda e, src=src, dst=dst: e.activation(
                dst, src, AF.Copy))
    xtb3 = st.xtb[:].rearrange("p (c n) -> p c n", c=8)
    if q == 0:
        ctx.op("dve", lambda e: e.memset(xtb3[:, :, 0:8], 0.0))
    if q == NQ - 1:
        ctx.op("dve", lambda e: e.memset(xtb3[:, :, 520:528], 0.0))
    dve_pads = ctx.n["dve"]

    nb_, nq_ = (b, q + 1) if q + 1 < NQ else (b + 1, 0)
    own_slots_now = {(4 * q + i) % 6 for i in range(4)}
    if nb_ < st.n_b:
        safe = [tc for tc in tc_needed(nq_)
                if (nb_, tc) not in st.loaded and tc % 6 not in own_slots_now]
        emit_x_loads(ctx, nb_, nq_, safe)

    if st.pending_ln is not None:
        pb_, pq_, prse = st.pending_ln
        st.pending_ln = None
        ln_finish_a(ctx, pb_, pq_, prse)

    xdep = [("act", None), ("dve", dve_pads)]
    for m in range(24):
        bank = 2 + m % 2
        tb = 4 + m // 12
        tcol = 16 * (m % 12)
        for k in range(8):
            lhs = wv(st.wqkv_sb, k, 3072, (128 * m, 128 * m + 128))
            deps = xdep if (k == 0 and m < 2) else []
            if m == 0 and k == 0:
                deps = deps + [("pl_dma", st.c_pl_wqkv)]
            ctx.mm(bank, lambda e, lhs=lhs, k=k, bank=bank: e.matmul(
                st.P[bank][:, 0:512], lhs, wv(st.xtb, k, 528, (0, 512)),
                start=k == 0, stop=k == 7), deps)
            ctx.mm(tb, lambda e, lhs=lhs, k=k, tb=tb, tcol=tcol: e.matmul(
                st.P[tb][:, tcol:tcol + 16], lhs, wv(st.xtb, k, 528, (512, 528)),
                start=k == 0, stop=k == 7))
        scale = 0.125 if m < 8 else 1.0
        ctx.rd("act", bank, lambda e, m=m, bank=bank, scale=scale: e.activation(
            wv(st.qkvb, m, 528, (0, 512)), st.P[bank][:, 0:512], AF.Copy,
            scale=scale))
    qkv3 = st.qkvb[:].rearrange("p (c n) -> p c n", c=24)
    ctx.rd("act", 4, lambda e: e.activation(
        qkv3[:, 0:8, 512:528],
        st.P[4][:, 0:128].rearrange("p (c n) -> p c n", c=8), AF.Copy, scale=0.125))
    ctx.rd("act", 4, lambda e: e.activation(
        qkv3[:, 8:12, 512:528],
        st.P[4][:, 128:192].rearrange("p (c n) -> p c n", c=4), AF.Copy))
    ctx.rd("act", 5, lambda e: e.activation(
        qkv3[:, 12:24, 512:528],
        st.P[5][:, 0:192].rearrange("p (c n) -> p c n", c=12), AF.Copy))
    st.act_qkv = ctx.n["act"]

    attention_quarter(ctx, q)

    f2v = st.f2[:].rearrange("p (c n) -> p c n", c=8)
    f1v = st.f1[:].rearrange("p (c n) -> p c n", c=8)
    proj_cnt = {}
    own = [(4 * q + i) % 6 for i in range(4)]
    otdep = [("act", None), ("dve", None)]
    for m in range(8):
        xb = 4 + m % 2
        for j, slot in enumerate(own):
            ctx.mm(xb, lambda e, slot=slot, m=m, j=j, xb=xb: e.transpose(
                st.P[xb][:, 128 * j:128 * j + 128],
                wv(st.xtok, slot, 1024, (128 * m, 128 * m + 128)),
                st.identf_sb[:]))
        ctx.rd("act", xb, lambda e, m=m, xb=xb: e.activation(
            f1v[:, m % 2, :], st.P[xb][:, 0:512], AF.Copy),
            [("dve", st.roll_user.get(m % 2, 0))])
        pb_ = 2 + m % 2
        for k in range(8):
            lhs = wv(st.wproj_sb, k, 1024, (128 * m, 128 * m + 128))
            ctx.mm(pb_, lambda e, lhs=lhs, k=k, pb_=pb_: e.matmul(
                st.P[pb_][:, 0:512], lhs, wv(st.otsb, k, 512),
                start=k == 0, stop=k == 7),
                otdep + [("pl_dma", st.c_pl_wproj)] if (k == 0 and m == 0) else [])
        st.roll_user[m % 2] = ctx.rd("dve", pb_, lambda e, m=m, pb_=pb_:
            e.scalar_tensor_tensor(
                f2v[:, m, :], st.P[pb_][:, 0:512], st.bproj_sb[:, m:m + 1],
                f1v[:, m % 2, :], ALU.add, ALU.add),
            [("act", None)] + ([("sp_dma", st.c_sp_bias)] if m == 0 else []))
        proj_cnt[m] = st.roll_user[m % 2]

    if nb_ < st.n_b:
        emit_x_loads(ctx, nb_, nq_, tc_needed(nq_), split=True)

    rse = ln_stats_rows_a(ctx, proj_cnt)
    st.pending_ln = (b, q, rse)

    if b == 0 and q == 1:
        ctx.q_serialize("pl_dma")
        wdma(ctx, "pl_dma", st.wcn1_sb[:], st.cn_w1)
        wdma(ctx, "pl_dma", st.wcn2_sb[:], st.cn_w2)
        wdma(ctx, "pl_dma", st.wcn3_sb[:], st.cn_w3)
        wdma(ctx, "pl_dma", st.wcn4_sb[:], st.cn_w4)
        wdma(ctx, "pl_dma", st.wgc1_sb[:], st.gc_w1)
        wdma(ctx, "pl_dma", st.wgc2_sb[:], st.gc_w2)
        wdma(ctx, "pl_dma", st.wgc3_sb[:], st.gc_w3)
        st.c_pl_wall = ctx.n["pl_dma"]


def attention_quarter(ctx, q):
    st = ctx.S
    chp = []
    for c in range(4):
        C = 4 * q + c
        wlo = max(0, 128 * C - 8)
        whi = min(SEQ, 128 * C + 136)
        chp.append((whi - wlo, wlo - (512 * q - 8), 128 * c + 8,
                    st.mask0_sb if C == 0 else st.maskM_sb, c))
    qkvdep = [("act", st.act_qkv)]
    pb_reader = ctx.S.pb_reader
    ts_cnt = {}
    vw_evac = {}
    exp_cnt = {}
    pt_evac = {}

    def s_front(h):
        W, wcol, qcol, mask, c = chp[h // 16]
        par = h % 2
        pp = h // 2
        sl6 = pp % 6
        bank = par + 4 * (sl6 // 3)
        co = 176 * (sl6 % 3)
        lhs = wv(st.qkvb, pp % 8, 528, (qcol, qcol + 128))[64 * par:64 * par + 64, :]
        rhs = wv(st.qkvb, 8 + pp % 8, 528, (wcol, wcol + W))[64 * par:64 * par + 64, :]
        ctx.mm(bank, lambda e, bank=bank: e.matmul(
            st.P[bank][:, co:co + W], lhs, rhs, start=True, stop=True))

    def stage1(pp):
        W, wcol, qcol, mask, c = chp[pp // 8]
        sl6 = pp % 6
        co = 176 * (sl6 % 3)
        for par in (0, 1):
            bank = par + 4 * (sl6 // 3)
            sl = 2 * (pp % 4) + par
            exp_cnt[pp] = ctx.rd("act", bank, lambda e, bank=bank, sl=sl: e.activation(
                st.pb[sl][:, 0:W], st.P[bank][:, co:co + W], AF.Exp),
                [("pe", pb_reader.get(sl, 0))])
            ctx.op("dve", lambda e, sl=sl, mask=mask, W=W: e.scalar_tensor_tensor(
                st.pmb[sl][:, 0:W], st.pb[sl][:, 0:W],
                1.0, mask[:, 0:W], ALU.mult, ALU.mult,
                accum_out=st.sacc[:, (sl % 2) + 2 * (pp % 2):(sl % 2) + 2 * (pp % 2) + 1]),
                [("act", exp_cnt[pp])])
        s0 = 2 * (pp % 2)
        ctx.op("dve", lambda e, s0=s0: e.reciprocal(
            st.sacc[:, 4 + s0:6 + s0], st.sacc[:, s0:s0 + 2]), [("dve", None)])
        for par in (0, 1):
            sl = 2 * (pp % 4) + par
            ts_cnt[pp] = ctx.op("dve", lambda e, sl=sl, s0=s0, par=par, W=W: e.tensor_scalar(
                st.pb[sl][:, 0:W], st.pmb[sl][:, 0:W],
                st.sacc[:, 4 + s0 + par:5 + s0 + par], None, ALU.mult),
                [("dve", None)])

    def stage2a(pp):
        W, wcol, qcol, mask, c = chp[pp // 8]
        vslot = pp % 4
        vbank = 6 + pp % 2
        vsrc = wv(st.qkvb, 16 + pp % 8, 528)
        ctx.mm(vbank, lambda e, vbank=vbank, vsrc=vsrc, wcol=wcol: e.transpose(
            st.Pb[vbank][:, 256:384], vsrc[:, wcol:wcol + 128], st.identb_sb[:]),
            qkvdep)
        ctx.mm(vbank, lambda e, vbank=vbank, vsrc=vsrc, wcol=wcol, W=W: e.transpose(
            st.Pb[vbank][0:W - 128, 384:512], vsrc[:, wcol + 128:wcol + W],
            st.identb_sb[:]))
        vw_evac[pp] = ctx.rd("act", vbank, lambda e, vbank=vbank, vslot=vslot: e.activation(
            wv(st.vwin, vslot, 256), st.Pb[vbank][:, 256:512], AF.Copy))
        for par in (0, 1):
            sl = 2 * (pp % 4) + par
            ptb = st.Pb[2 + par]
            ctx.mm(2 + par, lambda e, ptb=ptb, sl=sl: e.transpose(
                ptb[:, 0:128], st.pb[sl][:, 0:128], st.identb_sb[:]),
                [("dve", ts_cnt[pp])] if par == 0 else [])
            ctx.mm(2 + par, lambda e, ptb=ptb, sl=sl, W=W: e.transpose(
                ptb[0:W - 128, 128:256], st.pb[sl][:, 128:W], st.identb_sb[:]))
            pb_reader[sl] = ctx.n["pe"]
        for par in (0, 1):
            pt_evac[pp] = ctx.rd("dve", 2 + par, lambda e, par=par, pp=pp:
                e.tensor_copy(wv(st.ptsb, 2 * (pp % 3) + par, 256),
                              st.Pb[2 + par][:, 0:256]))

    def stage2b(pp):
        W, wcol, qcol, mask, c = chp[pp // 8]
        vslot = pp % 4
        for par in (0, 1):
            ob = 6 + par
            va = wv(st.vwin, vslot, 256, (64 * par, 64 * par + 64))
            vb = wv(st.vwin, vslot, 256, (128 + 64 * par, 128 + 64 * par + 64))
            ctx.mm(ob, lambda e, par=par, va=va, ob=ob, pp=pp: e.matmul(
                st.P[ob][0:64, 0:128], va,
                wv(st.ptsb, 2 * (pp % 3) + par, 256, (0, 128)),
                start=True, stop=False),
                [("dve", pt_evac[pp]), ("act", vw_evac[pp])] if par == 0 else [])
            ctx.mm(ob, lambda e, par=par, vb=vb, ob=ob, W=W, pp=pp: e.matmul(
                st.P[ob][0:64, 0:128], vb[0:W - 128, :],
                wv(st.ptsb, 2 * (pp % 3) + par, 256, (128, 256))[0:W - 128, :],
                start=False, stop=True))
        for par in (0, 1):
            ob = 6 + par
            ctx.rd("act", ob, lambda e, par=par, ob=ob, pp=pp, c=c: e.activation(
                wv(st.otsb, pp % 8, 512, (128 * c, 128 * c + 128))[64 * par:64 * par + 64, :],
                st.P[ob][0:64, 0:128], AF.Copy))

    D1, D2A, D2B = 3, 6, 9
    for step in range(32 + D2B):
        if step >= D2B:
            stage2b(step - D2B)
        if D2A <= step < 32 + D2A:
            stage2a(step - D2A)
        if D1 <= step < 32 + D1:
            stage1(step - D1)
        if step < 32:
            s_front(2 * step)
            s_front(2 * step + 1)


def ln_stats(ctx, src, src_cnt, banks, sq):
    st = ctx.S
    b1, b2 = banks
    sqc = {}
    mm2c = {}
    for m in range(8):
        sdep = [("dve", src_cnt[m] if src_cnt else None)]
        sqc[m] = ctx.op("act", lambda e, m=m: e.activation(
            sq[m % 2], src[:, m, :], AF.Square),
            sdep + ([("pe", None)] if m < 2 else [("pe", mm2c[m - 2])]))
        ctx.mm(b1, lambda e, m=m: e.matmul(
            st.P[b1][0:1, 0:512], st.onescol_sb[:], src[:, m, :],
            start=m == 0, stop=m == 7), sdep)
        mm2c[m] = ctx.mm(b2, lambda e, m=m: e.matmul(
            st.P[b2][0:1, 0:512], st.onescol_sb[:], sq[m % 2],
            start=m == 0, stop=m == 7), [("act", sqc[m])])
    return mm2c


def ln_rows(ctx, rows, t0, banks, mm2c7=None):
    st = ctx.S
    b1, b2 = banks
    mrow = rows[0:1, 0:512]
    rrow = rows[0:1, 512:1024]
    ctx.rd("dve", b1, lambda e: e.tensor_scalar(mrow, st.P[b1][0:1, 0:512],
                                                1.0 / D, None, ALU.mult))
    ctx.rd("dve", b2, lambda e: e.tensor_scalar(t0, st.P[b2][0:1, 0:512],
                                                1.0 / D, None, ALU.mult),
           [("pe", mm2c7)] if mm2c7 else [])
    ctx.op("dve", lambda e: e.tensor_tensor(rrow, mrow, mrow, ALU.mult),
           [("dve", None)])
    ctx.op("dve", lambda e: e.tensor_tensor(t0, t0, rrow, ALU.subtract),
           [("dve", None)])
    lnc = ctx.op("act", lambda e: e.activation(rrow, t0, AF.Ln,
        bias=st.srow[0:1, SL_EPS:SL_EPS + 1]), [("dve", None)])
    rse = ctx.op("act", lambda e: e.activation(rrow, rrow, AF.Exp, scale=-0.5),
                 [("act", lnc)])
    return rse


def ln_bcast(ctx, rows, banks, rse):
    st = ctx.S
    b1, b2 = banks
    ctx.mm(b1, lambda e: e.matmul(st.P[b1][:, 0:512], st.onesrow_sb[:],
                                  rows[0:1, 0:512], start=True, stop=True),
           [("dve", None)])
    ctx.mm(b2, lambda e: e.matmul(st.P[b2][:, 0:512], st.onesrow_sb[:],
                                  rows[0:1, 512:1024], start=True, stop=True),
           [("act", rse)])


def ln_norm(ctx, src, dst, tsc, banks, rse):
    st = ctx.S
    b1, b2 = banks
    norm_cnt = {}
    for m in range(8):
        ctx.rd("dve", b1, lambda e, m=m: e.tensor_tensor(
            tsc[:], src[:, m, :], st.P[b1][:, 0:512], ALU.subtract),
            [("act", rse), ("dve", None)] if m == 0 else [("dve", None)])
        norm_cnt[m] = ctx.rd("dve", b2, lambda e, m=m: e.scalar_tensor_tensor(
            dst[:, m, :], tsc[:], 1.0, st.P[b2][:, 0:512],
            ALU.mult, ALU.mult), [("dve", None)])
    return norm_cnt


def ln_stats_rows_a(ctx, proj_cnt):
    st = ctx.S
    src = st.f2[:].rearrange("p (c n) -> p c n", c=8)
    mm2c = ln_stats(ctx, src, proj_cnt, (6, 7), st.sq_a)
    return ln_rows(ctx, st.rows_a, st.tsc_a[0:1, 0:512], (6, 7), mm2c[7])


def ln_finish_a(ctx, b, q, rse):
    st = ctx.S
    b1, b2 = 6, 7
    src = st.f2[:].rearrange("p (c n) -> p c n", c=8)
    tsc = st.tsc_a
    mrow = st.rows_a[0:1, 0:512]
    rrow = st.rows_a[0:1, 512:1024]
    ctx.mm(b1, lambda e: e.matmul(st.P[b1][:, 0:512], st.onesrow_sb[:], mrow,
                                  start=True, stop=True))
    ctx.mm(b2, lambda e: e.matmul(st.P[b2][:, 0:512], st.onesrow_sb[:], rrow,
                                  start=True, stop=True), [("act", rse)])
    x1v = st.x1roll.rearrange("p (c n) -> p c n", c=2)
    spill_cnt = {}
    for m in range(8):
        ctx.rd("dve", b1, lambda e, m=m: e.tensor_tensor(
            tsc[:], src[:, m, :], st.P[b1][:, 0:512], ALU.subtract),
            [("dve", None)])
        acc = wv(st.gcparts, m, 16, (4 * b + q, 4 * b + q + 1))
        deps = ([("sp_dma", spill_cnt[m - 2])] if m >= 2
                else ([("sp_dma", st.spill_last[m % 2])]
                      if m % 2 in st.spill_last else []))
        ctx.rd("dve", b2, lambda e, m=m, acc=acc: e.scalar_tensor_tensor(
            x1v[:, m % 2, :], tsc[:], 1.0, st.P[b2][:, 0:512],
            ALU.mult, ALU.mult, accum_out=acc), deps + [("dve", None)])
        ctx.q_serialize("sp_dma")
        spill_cnt[m] = ctx.op("sp_dma", lambda e, m=m: e.dma_start(
            out=st.x1sp[b, m, :, 512 * q:512 * q + 512], in_=x1v[:, m % 2, :]),
            deps=[("dve", None)])
        st.spill_last[m % 2] = spill_cnt[m]
    st.spill_q[(b, q)] = spill_cnt[7]


def phase_b(ctx):
    st = ctx.S
    gp = st.gcparts[:].rearrange("p (c n) -> p c n", c=8)
    gs = st.gcsum[:].rearrange("p (b n) -> p b n", b=4)
    gb = st.gcbf[:].rearrange("p (b n) -> p b n", b=4)
    for b in range(st.n_b):
        dep = [("dve", None), ("sp_dma", st.c_sp_bias)] if b == 0 else []
        ctx.op("dve", lambda e, b=b: e.tensor_tensor(
            gs[:, b, :], gp[:, :, 4 * b + 0], gp[:, :, 4 * b + 1], ALU.add),
            dep)
        ctx.op("dve", lambda e, b=b: e.tensor_tensor(
            gs[:, b, :], gs[:, b, :], gp[:, :, 4 * b + 2], ALU.add),
            [("dve", None)])
        ctx.op("dve", lambda e, b=b: e.tensor_tensor(
            gs[:, b, :], gs[:, b, :], gp[:, :, 4 * b + 3], ALU.add),
            [("dve", None)])
        ctx.op("dve", lambda e, b=b: e.tensor_scalar(
            gs[:, b, :], gs[:, b, :], 1.0 / SEQ, None, ALU.mult),
            [("dve", None)])
        ctx.op("dve", lambda e, b=b: e.tensor_copy(gb[:, b, :], gs[:, b, :]),
               [("dve", None)])
    for m in range(4):
        for b in range(st.n_b):
            for k in range(8):
                lhs = wv(st.wgc1_sb, k, 512, (128 * m, 128 * m + 128))
                ctx.mm(b, lambda e, lhs=lhs, k=k, m=m, b=b: e.matmul(
                    st.P[b][:, m:m + 1], lhs, gb[:, b, k:k + 1],
                    start=k == 0, stop=k == 7),
                    [("dve", None), ("pl_dma", st.c_pl_wall)]
                    if (m == 0 and k == 0) else [])
        for b in range(st.n_b):
            ctx.rd("act", b, lambda e, m=m, b=b: e.activation(
                st.g1bf[:, 4 * b + m:4 * b + m + 1], st.P[b][:, m:m + 1],
                AF.Gelu, bias=st.bgc1_sb[:, m:m + 1]))
    for m in range(2):
        for b in range(st.n_b):
            for k in range(4):
                lhs = wv(st.wgc2_sb, k, 256, (128 * m, 128 * m + 128))
                ctx.mm(b, lambda e, lhs=lhs, k=k, m=m, b=b: e.matmul(
                    st.P[b][:, 128 + m:129 + m], lhs,
                    st.g1bf[:, 4 * b + k:4 * b + k + 1],
                    start=k == 0, stop=k == 3),
                    [("act", None)] if (m == 0 and b == 0) else [])
        for b in range(st.n_b):
            ctx.rd("act", b, lambda e, m=m, b=b: e.activation(
                st.g2bf[:, 2 * b + m:2 * b + m + 1],
                st.P[b][:, 128 + m:129 + m],
                AF.Gelu, bias=st.bgc2_sb[:, m:m + 1]))
    for b in range(st.n_b):
        for k in range(2):
            ctx.mm(b, lambda e, k=k, b=b: e.matmul(
                st.P[b][0:1, 256:257], st.wgc3_sb[:, k:k + 1],
                st.g2bf[:, 2 * b + k:2 * b + k + 1],
                start=k == 0, stop=k == 1),
                [("act", None)] if (k == 0 and b == 0) else [])
    for b in range(st.n_b):
        ctx.rd("act", b, lambda e, b=b: e.activation(
            st.srow[0:1, 20 + b:21 + b], st.P[b][0:1, 256:257], AF.Tanh,
            scale=0.5, bias=st.srow[0:1, SL_HB3:SL_HB3 + 1]))
    for b in range(st.n_b):
        ctx.op("dve", lambda e, b=b: e.tensor_scalar(
            st.srow[0:1, SL_CB2 + b:SL_CB2 + b + 1], st.srow[0:1, 20 + b:21 + b],
            st.srow[0:1, SL_E7:SL_E7 + 1], 0.1, ALU.mult, ALU.add),
            [("act", None)] + ([("dve", None)] if b == 0 else []))


TW_SRC = [4, 5, 6, 7, 0, 1, 2, 3]
TW_SGN = [1.0, 1.0, -1.0, -1.0, -1.0, -1.0, 1.0, 1.0]


def c_views(st, side):
    if side == 0:
        return dict(
            x1=st.xtok[:, 2048:6144].rearrange("p (c n) -> p c n", c=8),
            s0=st.f1[:].rearrange("p (c n) -> p c n", c=8),
            s1=st.f2[:].rearrange("p (c n) -> p c n", c=8),
            obv=st.qkvb[:, 7168:11264].rearrange("p (c n) -> p c n", c=8),
            tsc=st.tsc_c, rows=st.rows_c)
    return dict(
        x1=st.x1B.rearrange("p (c n) -> p c n", c=8),
        s0=st.sb0.rearrange("p (c n) -> p c n", c=8),
        s1=st.sb1.rearrange("p (c n) -> p c n", c=8),
        obv=st.obvB.rearrange("p (c n) -> p c n", c=8),
        tsc=st.tscB, rows=st.rowsB)


def emit_c_reload(ctx, b, q, vw_):
    st = ctx.S
    ctx.q_serialize("sp_dma")
    for m in range(8):
        ctx.op("sp_dma", lambda e, m=m: e.dma_start(
            out=vw_["x1"][:, m, :], in_=st.x1sp[b, m, :, 512 * q:512 * q + 512]),
            deps=[("dve", None), ("pe", None), ("act", None), ("pl", None)])
    st.c_rld[(b, q)] = ctx.n["sp_dma"]


def emit_c_obv_load(ctx, b, q, vw_):
    st = ctx.S
    ctx.q_serialize("pl_dma")
    for m in range(8):
        ctx.op("pl_dma", lambda e, m=m: e.dma_start(
            out=vw_["obv"][:, m, :], in_=st.x1sp[b, m, :, 512 * q:512 * q + 512]),
            deps=[("pe", None), ("act", None), ("sp_dma", st.spill_q[(b, q)])]
            if m == 0 else [])
    cnt = ctx.n["pl_dma"]
    return {m: ("pl_dma", cnt) for m in range(8)}


def c_chain(ctx, b, vw_, obv_cnt, cb):
    st = ctx.S
    h1v = st.qkvb[:, 0:4096].rearrange("p (c n) -> p c n", c=8)
    h2v = st.qkvb[:, 4096:6144].rearrange("p (c n) -> p c n", c=4)
    h3v = st.qkvb[:, 6144:7168].rearrange("p (c n) -> p c n", c=2)
    obv = vw_["obv"]
    throw_ = vw_["rows"][0:1, 0:512]
    crow = vw_["rows"][0:1, 512:1024]
    h1e = {}
    for half in range(2):
        for k in range(8):
            for mi in range(4):
                m = 4 * half + mi
                ctx.mm(mi, lambda e, m=m, k=k, mi=mi: e.matmul(
                    st.P[mi][:, 0:512],
                    wv(st.wcn1_sb, k, 1024, (128 * m, 128 * m + 128))
                    if False else st.wcn1v[k][:, 128 * m:128 * m + 128],
                    obv[:, k, :], start=k == 0, stop=k == 7),
                    [obv_cnt[k]] if mi == 0 and half == 0 else [])
        for mi in range(4):
            m = 4 * half + mi
            h1e[m] = ctx.rd("act", mi, lambda e, m=m, mi=mi: e.activation(
                h1v[:, m, :], st.P[mi][:, 0:512], AF.Gelu,
                bias=st.bcn1_sb[:, m:m + 1]))
    for k in range(8):
        for m in range(4):
            ctx.mm(m, lambda e, m=m, k=k: e.matmul(
                st.P[m][:, 0:512], st.wcn2v[k][:, 128 * m:128 * m + 128],
                h1v[:, k, :], start=k == 0, stop=k == 7),
                [("act", h1e[k])] if m == 0 else [])
    h2e = {}
    for m in range(4):
        h2e[m] = ctx.rd("act", m, lambda e, m=m: e.activation(
            h2v[:, m, :], st.P[m][:, 0:512], AF.Gelu, bias=st.bcn2_sb[:, m:m + 1]))
    for k in range(4):
        for m in range(2):
            ctx.mm(m, lambda e, m=m, k=k: e.matmul(
                st.P[m][:, 0:512], st.wcn3v[k][:, 128 * m:128 * m + 128],
                h2v[:, k, :], start=k == 0, stop=k == 3),
                [("act", h2e[k])] if m == 0 else [])
    h3e = {}
    for m in range(2):
        h3e[m] = ctx.rd("act", m, lambda e, m=m: e.activation(
            h3v[:, m, :], st.P[m][:, 0:512], AF.Gelu, bias=st.bcn3_sb[:, m:m + 1]))
    for k in range(2):
        ctx.mm(4, lambda e, k=k: e.matmul(
            st.P[4][0:1, 0:512], st.wcn4_sb[:, k:k + 1], h3v[:, k, :],
            start=k == 0, stop=k == 1), [("act", h3e[k])])
    ctx.rd("act", 4, lambda e: e.activation(
        throw_, st.P[4][0:1, 0:512], AF.Tanh, scale=0.5,
        bias=st.srow[0:1, SL_HB4:SL_HB4 + 1]))
    ctx.op("dve", lambda e: e.tensor_scalar(
        crow, throw_, st.srow[0:1, SL_A2:SL_A2 + 1],
        st.srow[0:1, SL_CB2 + b:SL_CB2 + b + 1], ALU.mult, ALU.add),
        [("act", None), ("dve", None)])
    ctx.mm(cb, lambda e: e.matmul(st.P[cb][:, 0:512], st.onesrow_sb[:], crow,
                                  start=True, stop=True), [("dve", None)])


def c_twist(ctx, vw_, cyc, curv, nxtv, obv_cnt, cb, extra_dep=()):
    st = ctx.S
    tsc = vw_["tsc"]
    for m in range(8):
        d = list(extra_dep) if m == 0 else []
        ctx.rd("dve", cb, lambda e, m=m: e.scalar_tensor_tensor(
            tsc[:], st.P[cb][:, 0:512], TW_SGN[m], curv[:, TW_SRC[m], :],
            ALU.mult, ALU.mult), d + [("dve", None)])
        adv = ctx.op("dve", lambda e, m=m: e.tensor_tensor(
            nxtv[:, m, :], curv[:, m, :], tsc[:], ALU.add), [("dve", None)])
        if cyc < 2:
            obv_cnt[m] = ("pl", ctx.op("pl", lambda e, m=m: e.tensor_copy(
                vw_["obv"][:, m, :], nxtv[:, m, :]), [("dve", adv)]))


def c_out(ctx, b, q, srcv, norm_cnt, side):
    st = ctx.S
    ctx.q_serialize("act_dma")
    for m in range(8):
        ctx.op("act_dma", lambda e, m=m: e.dma_start(
            out=st.out[b, m, :, 512 * q:512 * q + 512], in_=srcv[:, m, :]),
            deps=[("dve", norm_cnt[m])])
    st.out_tr[side] = ("act_dma", ctx.n["act_dma"])



def phase_c_pair(ctx, b, qa):
    st = ctx.S
    vwA = c_views(st, 0)
    vwB = c_views(st, 1)
    first_dep = [("pe", None), ("dve", None), ("act", None)]

    cA = st.c_pre.pop((0, b, qa), None)
    if cA is None:
        if (b, qa) not in st.c_rld:
            emit_c_reload(ctx, b, qa, vwA)
        cA = st.c_obv_pre.pop((b, qa), None) or             emit_c_obv_load(ctx, b, qa, vwA)
        c_chain(ctx, b, vwA, cA, 5)
    cB = st.c_pre.pop((1, b, qa + 1), None)
    if cB is None:
        if (b, qa + 1) not in st.c_rld:
            emit_c_reload(ctx, b, qa + 1, vwB)
        cB = st.c_obv_pre.pop((b, qa + 1), None) or             emit_c_obv_load(ctx, b, qa + 1, vwB)
        c_chain(ctx, b, vwB, cB, 7)
    nb_, nqa = (b, qa + 2) if qa + 2 < NQ else (b + 1, 0)
    have_next = nb_ < st.n_b
    otlA = ([st.out_tr[0]] if st.out_tr[0] else [("pe", None)]) + \
        [("sp_dma", st.c_rld[(b, qa)])]
    otlB = ([st.out_tr[1]] if st.out_tr[1] else [("pe", None)]) + \
        [("sp_dma", st.c_rld[(b, qa + 1)])]
    c_twist(ctx, vwA, 0, vwA["x1"], vwA["s0"], cA, 5, extra_dep=otlA)
    c_twist(ctx, vwB, 0, vwB["x1"], vwB["s0"], cB, 7, extra_dep=otlB)
    c_chain(ctx, b, vwA, cA, 5)
    c_twist(ctx, vwA, 1, vwA["s0"], vwA["s1"], cA, 5)
    c_chain(ctx, b, vwB, cB, 7)
    c_twist(ctx, vwB, 1, vwB["s0"], vwB["s1"], cB, 7)
    c_chain(ctx, b, vwA, cA, 5)
    c_twist(ctx, vwA, 2, vwA["s1"], vwA["s0"], cA, 5)
    if have_next:
        cNA = emit_c_obv_load(ctx, nb_, nqa, vwA)
    c_chain(ctx, b, vwB, cB, 7)
    c_twist(ctx, vwB, 2, vwB["s1"], vwB["s0"], cB, 7)
    if have_next:
        cNB = emit_c_obv_load(ctx, nb_, nqa + 1, vwB)
    resA = {}
    resB = {}
    for m in range(8):
        resA[m] = ctx.op("dve", lambda e, m=m: e.tensor_tensor(
            vwA["s1"][:, m, :], vwA["x1"][:, m, :], vwA["s0"][:, m, :], ALU.add),
            [("dve", None)] if m == 0 else [])
    for m in range(8):
        resB[m] = ctx.op("dve", lambda e, m=m: e.tensor_tensor(
            vwB["s1"][:, m, :], vwB["x1"][:, m, :], vwB["s0"][:, m, :], ALU.add),
            [("dve", None)] if m == 0 else [])
    if have_next:
        emit_c_reload(ctx, nb_, nqa, vwA)
        emit_c_reload(ctx, nb_, nqa + 1, vwB)
        c_chain(ctx, nb_, vwA, cNA, 5)
        c_chain(ctx, nb_, vwB, cNB, 7)
        st.c_pre[(0, nb_, nqa)] = cNA
        st.c_pre[(1, nb_, nqa + 1)] = cNB
    ln_stats(ctx, vwA["s1"], resA, (0, 1), st.sq_cA)
    ln_stats(ctx, vwB["s1"], resB, (2, 3), st.sq_cB)
    rseA = ln_rows(ctx, vwA["rows"], vwA["tsc"][0:1, 0:512], (0, 1))
    rseB = ln_rows(ctx, vwB["rows"], vwB["tsc"][0:1, 0:512], (2, 3))
    ln_bcast(ctx, vwA["rows"], (0, 1), rseA)
    ln_bcast(ctx, vwB["rows"], (2, 3), rseB)
    nA = ln_norm(ctx, vwA["s1"], vwA["s0"], vwA["tsc"], (0, 1), rseA)
    c_out(ctx, b, qa, vwA["s0"], nA, 0)
    nB = ln_norm(ctx, vwB["s1"], vwB["s0"], vwB["tsc"], (2, 3), rseB)
    c_out(ctx, b, qa + 1, vwB["s0"], nB, 1)


def build_nc(n_b=NB):
    nc = bass.Bass("TRN2", target_bir_lowering=False)
    st = build_state(nc, n_b)
    blk = st.block
    for name in ("sp", "pl", "dve", "act", "pe"):
        program(Ctx(st, name, None, dry=True))
    for name, attr in (("sp", "sync"), ("pl", "gpsimd"), ("dve", "vector"),
                       ("act", "scalar"), ("pe", "tensor")):
        def make(name):
            def run(eng):
                program(Ctx(st, name, eng))
            return run
        getattr(blk, attr)(make(name))
    st.ex.close()
    return nc


def kernel(L0_attn_x, L0_attn_Wqkv, L0_attn_Wproj, L0_attn_bproj,
           L0_ffn_cn_w1, L0_ffn_cn_b1, L0_ffn_cn_w2, L0_ffn_cn_b2,
           L0_ffn_cn_w3, L0_ffn_cn_b3, L0_ffn_cn_w4, L0_ffn_cn_b4,
           L0_ffn_gc_w1, L0_ffn_gc_b1, L0_ffn_gc_w2, L0_ffn_gc_b2,
           L0_ffn_gc_w3, L0_ffn_gc_b3, L0_ffn_adaptive_range):
    x = np.ascontiguousarray(np.asarray(L0_attn_x, dtype=np.float32))
    shared = {
        "wqkv": np.ascontiguousarray(L0_attn_Wqkv, dtype=np.float32),
        "wproj": np.ascontiguousarray(L0_attn_Wproj, dtype=np.float32),
        "bproj": np.ascontiguousarray(L0_attn_bproj, dtype=np.float32),
        "cn_w1": np.ascontiguousarray(L0_ffn_cn_w1, dtype=np.float32),
        "cn_b1": np.ascontiguousarray(L0_ffn_cn_b1, dtype=np.float32),
        "cn_w2": np.ascontiguousarray(L0_ffn_cn_w2, dtype=np.float32),
        "cn_b2": np.ascontiguousarray(L0_ffn_cn_b2, dtype=np.float32),
        "cn_w3": np.ascontiguousarray(L0_ffn_cn_w3, dtype=np.float32),
        "cn_b3": np.ascontiguousarray(L0_ffn_cn_b3, dtype=np.float32),
        "cn_w4": np.ascontiguousarray(L0_ffn_cn_w4, dtype=np.float32),
        "cn_b4": np.ascontiguousarray(L0_ffn_cn_b4, dtype=np.float32),
        "gc_w1": np.ascontiguousarray(L0_ffn_gc_w1, dtype=np.float32),
        "gc_b1": np.ascontiguousarray(L0_ffn_gc_b1, dtype=np.float32),
        "gc_w2": np.ascontiguousarray(L0_ffn_gc_w2, dtype=np.float32),
        "gc_b2": np.ascontiguousarray(L0_ffn_gc_b2, dtype=np.float32),
        "gc_w3": np.ascontiguousarray(L0_ffn_gc_w3, dtype=np.float32),
        "gc_b3": np.ascontiguousarray(L0_ffn_gc_b3, dtype=np.float32),
        "ar": np.ascontiguousarray(L0_ffn_adaptive_range, dtype=np.float32),
        **_const_inputs(),
    }
    nc = build_nc(NB)
    in_maps = [dict(shared, x=np.ascontiguousarray(x[4 * c:4 * c + 4]))
               for c in range(N_CORE)]
    res = run_bass_kernel_spmd(nc, in_maps, core_ids=list(range(N_CORE)))
    arr = np.concatenate([r["out"] for r in res.results], axis=0)
    return np.ascontiguousarray(
        arr.transpose(0, 3, 1, 2).reshape(32 * N_CORE // 8, SEQ, D))


# version 43
# speedup vs baseline: 1.0025x; 1.0025x over previous
import numpy as np
import ml_dtypes
from contextlib import ExitStack

import concourse.bass as bass
import concourse.mybir as mybir
import concourse.bass_utils as _bu
from concourse.bass_utils import run_bass_kernel_spmd

_orig_run_command = _bu.run_command


def _patched_run_command(argv, **kwargs):
    argv = ["--enable-ldw-opt=true" if a == "--enable-ldw-opt=false" else a
            for a in argv]
    return _orig_run_command(argv, **kwargs)


if _bu.run_command is not _patched_run_command:
    _bu.run_command = _patched_run_command

F32 = mybir.dt.float32
BF16 = mybir.dt.bfloat16
AF = mybir.ActivationFunctionType
ALU = mybir.AluOpType

N_CORE = 8
NB = 4
SEQ = 2048
D = 1024
NQ = 4
NH = 16
EPS = 1e-5

ENG_OF = {"sp_dma": "sp", "pl_dma": "pl", "act_dma": "act"}


def _const_inputs():
    p = np.arange(128)[:, None]
    w = np.arange(144)[None, :]
    return {
        "ident_f": np.eye(128, dtype=np.float32),
        "ident_b": np.eye(128, dtype=ml_dtypes.bfloat16),
        "ones_row": np.ones((1, 128), dtype=np.float32),
        "ones_col": np.ones((128, 1), dtype=np.float32),
        "mask0": (np.abs(p - w) <= 8).astype(ml_dtypes.bfloat16),
        "maskM": ((w >= p) & (w <= p + 16)).astype(ml_dtypes.bfloat16),
    }


class S:
    pass


class Ctx:

    def __init__(self, st, cur, eng):
        self.S = st
        self.cur = cur
        self.e = eng
        self.n = {k: 0 for k in ("pe", "dve", "act", "sp", "pl", "sp_dma", "pl_dma", "act_dma")}
        self.waited = {}
        self.bank_w = {}
        self.bank_r = {}

    def _wait(self, key, cnt):
        if ENG_OF.get(key, key) == self.cur:
            return
        val = cnt * (16 if key.endswith("_dma") else 1)
        if val <= 0 or self.waited.get(key, 0) >= val:
            return
        self.waited[key] = val
        self.e.wait_ge(self.S.sems[key], val)

    def raw_wait(self, key, cnt=None):
        c = self.n[key] if cnt is None else cnt
        val = c * (16 if key.endswith("_dma") else 1)
        if val <= 0 or self.waited.get(key, 0) >= val:
            return
        self.waited[key] = val
        self.e.wait_ge(self.S.sems[key], val)

    def q_serialize(self, qkey):
        if ENG_OF[qkey] == self.cur:
            self.raw_wait(qkey, self.n[qkey])

    def op(self, key, fn, deps=()):
        eng = ENG_OF.get(key, key)
        if eng == self.cur:
            for (tk, tc) in deps:
                cnt = self.n[tk] if tc is None else tc
                if ENG_OF.get(tk, tk) == self.cur:
                    self.raw_wait(tk, cnt)
                else:
                    self._wait(tk, cnt)
        self.n[key] += 1
        if eng == self.cur:
            fn(self.e).then_inc(self.S.sems[key], 16 if key.endswith("_dma") else 1)
        return self.n[key]

    def mm(self, bank, fn, deps=()):
        d = list(deps)
        rs = self.bank_r.get(bank)
        if rs:
            d += list(rs.items())
        n = self.op("pe", fn, d)
        self.bank_r[bank] = {}
        self.bank_w[bank] = n
        return n

    def rd(self, key, bank, fn, deps=()):
        d = list(deps) + [("pe", self.bank_w.get(bank, 0))]
        n = self.op(key, fn, d)
        self.bank_r.setdefault(bank, {})[key] = n
        return n


def wv(t, chunk, width, sub=None):
    lo = chunk * width + (sub[0] if sub else 0)
    hi = chunk * width + (sub[1] if sub else width)
    return t[:, lo:hi]


def build_state(nc, n_b):
    st = S()
    st.nc = nc
    st.n_b = n_b

    def din(name, shape, dtype=F32):
        return nc.dram_tensor(name, list(shape), dtype, kind="ExternalInput").ap()

    st.x = din("x", [n_b, SEQ, D])
    st.wqkv = din("wqkv", [D, 3 * D])
    st.wproj = din("wproj", [D, D])
    st.bproj = din("bproj", [D])
    st.cn_w1 = din("cn_w1", [D, D]); st.cn_b1 = din("cn_b1", [D])
    st.cn_w2 = din("cn_w2", [D, 512]); st.cn_b2 = din("cn_b2", [512])
    st.cn_w3 = din("cn_w3", [512, 256]); st.cn_b3 = din("cn_b3", [256])
    st.cn_w4 = din("cn_w4", [256, 1]); st.cn_b4 = din("cn_b4", [1])
    st.gc_w1 = din("gc_w1", [D, 512]); st.gc_b1 = din("gc_b1", [512])
    st.gc_w2 = din("gc_w2", [512, 256]); st.gc_b2 = din("gc_b2", [256])
    st.gc_w3 = din("gc_w3", [256, 1]); st.gc_b3 = din("gc_b3", [1])
    st.ar = din("ar", [1])
    st.ident_f = din("ident_f", [128, 128])
    st.ident_b = din("ident_b", [128, 128], BF16)
    st.ones_row = din("ones_row", [1, 128])
    st.ones_col = din("ones_col", [128, 1])
    st.mask0 = din("mask0", [128, 144], BF16)
    st.maskM = din("maskM", [128, 144], BF16)

    st.out = nc.dram_tensor("out", [n_b, 8, 128, SEQ], F32,
                        kind="ExternalOutput").ap()
    st.x1sp = nc.dram_tensor("x1sp", [n_b, 8, 128, SEQ], F32, kind="Internal").ap()

    st.ex = ExitStack()
    e = st.ex.enter_context

    def sb(name, shape, dtype):
        return e(nc.sbuf_tensor(name, list(shape), dtype))

    st.wqkv_sb = sb("wqkv_sb", [128, 8 * 3072], BF16)
    st.wproj_sb = sb("wproj_sb", [128, 8 * 1024], BF16)
    st.wcn1_sb = sb("wcn1_sb", [128, 8 * 1024], BF16)
    st.wcn2_sb = sb("wcn2_sb", [128, 8 * 512], BF16)
    st.wcn3_sb = sb("wcn3_sb", [128, 4 * 256], BF16)
    st.wcn4_sb = sb("wcn4_sb", [128, 2], BF16)
    st.wgc1_sb = sb("wgc1_sb", [128, 8 * 512], BF16)
    st.wgc2_sb = sb("wgc2_sb", [128, 4 * 256], BF16)
    st.wgc3_sb = sb("wgc3_sb", [128, 2], BF16)
    st.bproj_sb = sb("bproj_sb", [128, 8], F32)
    st.bcn1_sb = sb("bcn1_sb", [128, 8], F32)
    st.bcn2_sb = sb("bcn2_sb", [128, 4], F32)
    st.bcn3_sb = sb("bcn3_sb", [128, 2], F32)
    st.bgc1_sb = sb("bgc1_sb", [128, 4], F32)
    st.bgc2_sb = sb("bgc2_sb", [128, 2], F32)
    st.srow = sb("srow", [1, 64], F32)
    st.identf_sb = sb("identf_sb", [128, 128], F32)
    st.identb_sb = sb("identb_sb", [128, 128], BF16)
    st.onesrow_sb = sb("onesrow_sb", [1, 128], F32)
    st.onescol_sb = sb("onescol_sb", [128, 1], F32)
    st.mask0_sb = sb("mask0_sb", [128, 144], BF16)
    st.maskM_sb = sb("maskM_sb", [128, 144], BF16)

    st.xtok = sb("xtok", [128, 6 * 1024], F32)
    st.xtb = sb("xtb", [128, 8 * 528], BF16)
    st.qkvb = sb("qkvb", [128, 24 * 528], BF16)
    st.vwin = sb("vwin", [128, 4 * 256], BF16)
    st.ptsb = sb("ptsb", [128, 6 * 256], BF16)
    st.sacc = sb("sacc", [128, 8], F32)
    st.otsb = sb("otsb", [128, 8 * 512], BF16)
    st.f2 = sb("f2", [128, 8 * 512], F32)
    st.f1 = sb("f1", [128, 8 * 512], F32)
    st.gcparts = sb("gcparts", [128, 8 * 16], F32)
    st.gcsum = sb("gcsum", [128, 32], F32)
    st.gcbf = sb("gcbf", [128, 32], BF16)
    st.g1bf = sb("g1bf", [128, 16], BF16)
    st.g2bf = sb("g2bf", [128, 8], BF16)

    st.P = [e(nc.psum_tensor(f"P{i}", [128, 512], F32)) for i in range(8)]
    st.wcn1v = [wv(st.wcn1_sb, k, 1024) for k in range(8)]
    st.wcn2v = [wv(st.wcn2_sb, k, 512) for k in range(8)]
    st.wcn3v = [wv(st.wcn3_sb, k, 256) for k in range(4)]
    st.x1roll = st.f1[:, 1024:2048]
    st.tsc_a = st.f1[:, 2048:2560]
    st.rows_a = st.f1[0:1, 2560:3584]
    st.sq_a = [st.f1[:, 3584:4096], st.f1[:, 2048:2560]]
    xtbf32 = st.xtb[:].bitcast(F32)
    st.tsc_c = xtbf32[:, 0:512]
    st.sq_cB = [xtbf32[:, 512:1024], xtbf32[:, 1024:1536]]
    otsbf32 = st.otsb[:].bitcast(F32)
    st.rows_c = otsbf32[0:1, 1024:2048]
    st.sq_cA = [otsbf32[:, 0:512], otsbf32[:, 512:1024]]
    st.pb = [st.xtb[:, 1024 + 160 * i: 1024 + 160 * (i + 1)] for i in range(8)]
    st.pmb = [st.xtb[:, 2304 + 160 * i: 2304 + 160 * (i + 1)] for i in range(8)]
    wqf = st.wqkv_sb[:].bitcast(F32)
    st.sb0 = wqf[:, 0:4096]
    st.sb1 = wqf[:, 4096:8192]
    st.x1B = wqf[:, 8192:12288]
    st.obvB = st.wproj_sb[:, 0:4096]
    wpf = st.wproj_sb[:].bitcast(F32)
    st.tscB = wpf[:, 2048:2560]
    st.rowsB = wpf[0:1, 2560:3584]
    st.Pb = [st.P[i][:].bitcast(BF16) for i in range(8)]

    st.sems = {}
    for k in ("pe", "dve", "act", "sp", "pl", "sp_dma", "pl_dma", "act_dma"):
        st.sems[k] = e(nc.semaphore(f"sem_{k}"))
    st.block = e(nc.Block())
    return st


SL_A2, SL_E7, SL_HB4, SL_HB3, SL_THG, SL_EPS, SL_CB2 = 0, 1, 2, 3, 4, 5, 8


def program(ctx):
    st = ctx.S
    setup(ctx)
    for b in range(st.n_b):
        for q in range(NQ):
            phase_a(ctx, b, q)
    if st.pending_ln is not None:
        pb_, pq_, prse = st.pending_ln
        st.pending_ln = None
        ln_finish_a(ctx, pb_, pq_, prse)
    vwA = c_views(st, 0)
    vwB = c_views(st, 1)
    emit_c_reload(ctx, 0, 0, vwA)
    emit_c_reload(ctx, 0, 1, vwB)
    st.c_obv_pre = {(0, 0): emit_c_obv_load(ctx, 0, 0, vwA),
                    (0, 1): emit_c_obv_load(ctx, 0, 1, vwB)}
    phase_b(ctx)
    for b in range(st.n_b):
        for qa in range(0, NQ, 2):
            phase_c_pair(ctx, b, qa)
    if ctx.cur == "sp":
        ctx.raw_wait("sp_dma")
    if ctx.cur == "pl":
        ctx.raw_wait("pl_dma")
    if ctx.cur == "act":
        ctx.raw_wait("act_dma")


def wdma(ctx, qkey, dst, src):
    kc = src.shape[0] // 128
    slow = src.shape[1] < 16
    ctx.op(qkey, lambda e: e.dma_start(
        out=dst.rearrange("p (kc m) -> p kc m", kc=kc),
        in_=src.rearrange("(kc p) m -> p kc m", p=128),
        allow_slow_non_contiguous=slow))


def setup(ctx):
    st = ctx.S
    st.loaded = set()
    st.load_cnt = {}
    st.roll_user = {}
    st.spill_last = {}
    st.pb_reader = {}
    st.c_rld = {}
    st.out_dma_last = {}
    st.c_pre = {}
    st.out_tr = [None, None]
    st.spill_q = {}
    st.c_obv_pre = {}
    st.pending_ln = None

    def cdma(fn):
        ctx.op("sp_dma", fn)
    cdma(lambda e: e.dma_start(out=st.identf_sb[:], in_=st.ident_f))
    cdma(lambda e: e.dma_start(out=st.identb_sb[:], in_=st.ident_b))
    cdma(lambda e: e.dma_start(out=st.onesrow_sb[:], in_=st.ones_row))
    cdma(lambda e: e.dma_start(out=st.onescol_sb[:], in_=st.ones_col))
    cdma(lambda e: e.dma_start(out=st.mask0_sb[:], in_=st.mask0))
    cdma(lambda e: e.dma_start(out=st.maskM_sb[:], in_=st.maskM))
    cdma(lambda e: e.dma_start(out=st.srow[0:1, 16:17], in_=st.ar.unsqueeze(0)))
    cdma(lambda e: e.dma_start(out=st.srow[0:1, 17:18], in_=st.cn_b4.unsqueeze(0)))
    cdma(lambda e: e.dma_start(out=st.srow[0:1, 18:19], in_=st.gc_b3.unsqueeze(0)))
    st.c_sp_consts = ctx.n["sp_dma"]
    ctx.q_serialize("sp_dma")
    for tc in tc_needed(0):
        cdma(lambda e, tc=tc: e.dma_start(
            out=wv(st.xtok, tc % 6, 1024),
            in_=st.x[0, 128 * tc:128 * tc + 128, :]))
        st.loaded.add((0, tc))
    for tc in tc_needed(0):
        st.load_cnt[(0, tc)] = ("sp_dma", ctx.n["sp_dma"])
    ctx.q_serialize("sp_dma")
    for dst, src in ((st.bproj_sb, st.bproj), (st.bcn1_sb, st.cn_b1),
                     (st.bcn2_sb, st.cn_b2), (st.bcn3_sb, st.cn_b3),
                     (st.bgc1_sb, st.gc_b1), (st.bgc2_sb, st.gc_b2)):
        cdma(lambda e, dst=dst, src=src: e.dma_start(
            out=dst[:], in_=src.rearrange("(c p) -> p c", p=128),
            allow_slow_non_contiguous=True))
    st.c_sp_bias = ctx.n["sp_dma"]
    wdma(ctx, "pl_dma", st.wqkv_sb[:], st.wqkv)
    st.c_pl_wqkv = ctx.n["pl_dma"]
    ctx.q_serialize("pl_dma")
    wdma(ctx, "pl_dma", st.wproj_sb[:], st.wproj)
    st.c_pl_wproj = ctx.n["pl_dma"]
    ctx.q_serialize("pl_dma")

    dep = [("sp_dma", st.c_sp_consts)]
    r = st.srow
    ctx.op("dve", lambda e: e.tensor_scalar(
        r[0:1, SL_A2:SL_A2 + 1], r[0:1, 16:17], 0.3, None, ALU.mult), dep)
    ctx.op("dve", lambda e: e.tensor_scalar(
        r[0:1, SL_E7:SL_E7 + 1], r[0:1, 16:17], 0.7, None, ALU.mult))
    ctx.op("dve", lambda e: e.tensor_scalar(
        r[0:1, SL_HB4:SL_HB4 + 1], r[0:1, 17:18], 0.5, None, ALU.mult))
    ctx.op("dve", lambda e: e.tensor_scalar(
        r[0:1, SL_HB3:SL_HB3 + 1], r[0:1, 18:19], 0.5, None, ALU.mult))

    ctx.op("dve", lambda e: e.memset(r[0:1, SL_EPS:SL_EPS + 1], EPS))
    for i in (6, 7):
        ctx.op("dve", lambda e, i=i: e.memset(st.P[i][:, 192:256], 0.0))
    if ctx.cur in ("pe", "dve", "act"):
        ctx.raw_wait("sp_dma", st.c_sp_consts)
    if ctx.cur == "act":
        ctx._wait("dve", ctx.n["dve"])


def tc_needed(q):
    lo = max(0, 4 * q - 1)
    hi = min(16, 4 * q + 5)
    return list(range(lo, hi))


def emit_x_loads(ctx, b, q, tcs, split=False, pe_cnt=None):
    st = ctx.S
    todo = [tc for tc in tcs if (b, tc) not in st.loaded]
    if not todo:
        return
    qs = {tc: ("sp_dma" if split and i % 2 else "pl_dma")
          for i, tc in enumerate(todo)}
    for qk in set(qs.values()):
        ctx.q_serialize(qk)
    for tc in todo:
        ctx.op(qs[tc], lambda e, tc=tc: e.dma_start(
            out=wv(st.xtok, tc % 6, 1024),
            in_=st.x[b, 128 * tc:128 * tc + 128, :]),
            deps=[("pe", pe_cnt)])
        st.loaded.add((b, tc))
    for tc in todo:
        st.load_cnt[(b, tc)] = (qs[tc], ctx.n[qs[tc]])


def phase_a(ctx, b, q):
    st = ctx.S
    base = 512 * q - 8

    for tc in sorted(tc_needed(q), key=lambda tc: st.load_cnt[(b, tc)][1]):
        col0 = 128 * tc - base
        slot = tc % 6
        for f in range(8):
            bank = (f + 4 * tc) % 4
            deps = [st.load_cnt[(b, tc)]] if f == 0 else []
            ctx.mm(bank, lambda e, slot=slot, f=f, bank=bank: e.transpose(
                st.P[bank][:, 0:128],
                wv(st.xtok, slot, 1024, (128 * f, 128 * f + 128)),
                st.identf_sb[:]), deps)
            if col0 == -120:
                src = st.P[bank][:, 120:128]
                dst = wv(st.xtb, f, 528, (0, 8))
            elif col0 == 520:
                src = st.P[bank][:, 0:8]
                dst = wv(st.xtb, f, 528, (520, 528))
            else:
                src = st.P[bank][:, 0:128]
                dst = wv(st.xtb, f, 528, (col0, col0 + 128))
            ctx.rd("act", bank, lambda e, src=src, dst=dst: e.activation(
                dst, src, AF.Copy))
    xtb3 = st.xtb[:].rearrange("p (c n) -> p c n", c=8)
    if q == 0:
        ctx.op("dve", lambda e: e.memset(xtb3[:, :, 0:8], 0.0))
    if q == NQ - 1:
        ctx.op("dve", lambda e: e.memset(xtb3[:, :, 520:528], 0.0))
    dve_pads = ctx.n["dve"]

    nb_, nq_ = (b, q + 1) if q + 1 < NQ else (b + 1, 0)
    own_slots_now = {(4 * q + i) % 6 for i in range(4)}
    if nb_ < st.n_b:
        safe = [tc for tc in tc_needed(nq_)
                if (nb_, tc) not in st.loaded and tc % 6 not in own_slots_now]
        emit_x_loads(ctx, nb_, nq_, safe)

    if st.pending_ln is not None:
        pb_, pq_, prse = st.pending_ln
        st.pending_ln = None
        ln_finish_a(ctx, pb_, pq_, prse)

    xdep = [("act", None), ("dve", dve_pads)]
    for m in range(24):
        bank = 2 + m % 2
        tb = 4 + m // 12
        tcol = 16 * (m % 12)
        for k in range(8):
            lhs = wv(st.wqkv_sb, k, 3072, (128 * m, 128 * m + 128))
            deps = xdep if (k == 0 and m < 2) else []
            if m == 0 and k == 0:
                deps = deps + [("pl_dma", st.c_pl_wqkv)]
            ctx.mm(bank, lambda e, lhs=lhs, k=k, bank=bank: e.matmul(
                st.P[bank][:, 0:512], lhs, wv(st.xtb, k, 528, (0, 512)),
                start=k == 0, stop=k == 7), deps)
            ctx.mm(tb, lambda e, lhs=lhs, k=k, tb=tb, tcol=tcol: e.matmul(
                st.P[tb][:, tcol:tcol + 16], lhs, wv(st.xtb, k, 528, (512, 528)),
                start=k == 0, stop=k == 7))
        scale = 0.125 if m < 8 else 1.0
        ctx.rd("act", bank, lambda e, m=m, bank=bank, scale=scale: e.activation(
            wv(st.qkvb, m, 528, (0, 512)), st.P[bank][:, 0:512], AF.Copy,
            scale=scale))
    qkv3 = st.qkvb[:].rearrange("p (c n) -> p c n", c=24)
    ctx.rd("act", 4, lambda e: e.activation(
        qkv3[:, 0:8, 512:528],
        st.P[4][:, 0:128].rearrange("p (c n) -> p c n", c=8), AF.Copy, scale=0.125))
    ctx.rd("act", 4, lambda e: e.activation(
        qkv3[:, 8:12, 512:528],
        st.P[4][:, 128:192].rearrange("p (c n) -> p c n", c=4), AF.Copy))
    ctx.rd("act", 5, lambda e: e.activation(
        qkv3[:, 12:24, 512:528],
        st.P[5][:, 0:192].rearrange("p (c n) -> p c n", c=12), AF.Copy))
    st.act_qkv = ctx.n["act"]

    attention_quarter(ctx, q)

    f2v = st.f2[:].rearrange("p (c n) -> p c n", c=8)
    f1v = st.f1[:].rearrange("p (c n) -> p c n", c=8)
    proj_cnt = {}
    own = [(4 * q + i) % 6 for i in range(4)]
    otdep = [("act", None), ("dve", None)]
    for m in range(8):
        xb = 4 + m % 2
        for j, slot in enumerate(own):
            ctx.mm(xb, lambda e, slot=slot, m=m, j=j, xb=xb: e.transpose(
                st.P[xb][:, 128 * j:128 * j + 128],
                wv(st.xtok, slot, 1024, (128 * m, 128 * m + 128)),
                st.identf_sb[:]))
        if m == 7:
            st.pe_xfree = ctx.n["pe"]
        ctx.rd("act", xb, lambda e, m=m, xb=xb: e.activation(
            f1v[:, m % 2, :], st.P[xb][:, 0:512], AF.Copy),
            [("dve", st.roll_user.get(m % 2, 0))])
        pb_ = 2 + m % 2
        for k in range(8):
            lhs = wv(st.wproj_sb, k, 1024, (128 * m, 128 * m + 128))
            ctx.mm(pb_, lambda e, lhs=lhs, k=k, pb_=pb_: e.matmul(
                st.P[pb_][:, 0:512], lhs, wv(st.otsb, k, 512),
                start=k == 0, stop=k == 7),
                otdep + [("pl_dma", st.c_pl_wproj)] if (k == 0 and m == 0) else [])
        st.roll_user[m % 2] = ctx.rd("dve", pb_, lambda e, m=m, pb_=pb_:
            e.scalar_tensor_tensor(
                f2v[:, m, :], st.P[pb_][:, 0:512], st.bproj_sb[:, m:m + 1],
                f1v[:, m % 2, :], ALU.add, ALU.add),
            [("act", None)] + ([("sp_dma", st.c_sp_bias)] if m == 0 else []))
        proj_cnt[m] = st.roll_user[m % 2]

    if nb_ < st.n_b:
        emit_x_loads(ctx, nb_, nq_, tc_needed(nq_), split=True,
                     pe_cnt=st.pe_xfree)

    rse = ln_stats_rows_a(ctx, proj_cnt)
    st.pending_ln = (b, q, rse)

    if b == 0 and q == 1:
        ctx.q_serialize("pl_dma")
        wdma(ctx, "pl_dma", st.wcn1_sb[:], st.cn_w1)
        wdma(ctx, "pl_dma", st.wcn2_sb[:], st.cn_w2)
        wdma(ctx, "pl_dma", st.wcn3_sb[:], st.cn_w3)
        wdma(ctx, "pl_dma", st.wcn4_sb[:], st.cn_w4)
        wdma(ctx, "pl_dma", st.wgc1_sb[:], st.gc_w1)
        wdma(ctx, "pl_dma", st.wgc2_sb[:], st.gc_w2)
        wdma(ctx, "pl_dma", st.wgc3_sb[:], st.gc_w3)
        st.c_pl_wall = ctx.n["pl_dma"]


def attention_quarter(ctx, q):
    st = ctx.S
    chp = []
    for c in range(4):
        C = 4 * q + c
        wlo = max(0, 128 * C - 8)
        whi = min(SEQ, 128 * C + 136)
        chp.append((whi - wlo, wlo - (512 * q - 8), 128 * c + 8,
                    st.mask0_sb if C == 0 else st.maskM_sb, c))
    qkvdep = [("act", st.act_qkv)]
    pb_reader = ctx.S.pb_reader
    ts_cnt = {}
    vw_evac = {}
    exp_cnt = {}
    pt_evac = {}

    def s_front(h):
        W, wcol, qcol, mask, c = chp[h // 16]
        par = h % 2
        pp = h // 2
        sl6 = pp % 6
        bank = par + 4 * (sl6 // 3)
        co = 176 * (sl6 % 3)
        lhs = wv(st.qkvb, pp % 8, 528, (qcol, qcol + 128))[64 * par:64 * par + 64, :]
        rhs = wv(st.qkvb, 8 + pp % 8, 528, (wcol, wcol + W))[64 * par:64 * par + 64, :]
        ctx.mm(bank, lambda e, bank=bank: e.matmul(
            st.P[bank][:, co:co + W], lhs, rhs, start=True, stop=True))

    def stage1(pp):
        W, wcol, qcol, mask, c = chp[pp // 8]
        sl6 = pp % 6
        co = 176 * (sl6 % 3)
        for par in (0, 1):
            bank = par + 4 * (sl6 // 3)
            sl = 2 * (pp % 4) + par
            exp_cnt[pp] = ctx.rd("act", bank, lambda e, bank=bank, sl=sl: e.activation(
                st.pb[sl][:, 0:W], st.P[bank][:, co:co + W], AF.Exp),
                [("pe", pb_reader.get(sl, 0))])
            ctx.op("dve", lambda e, sl=sl, mask=mask, W=W: e.scalar_tensor_tensor(
                st.pmb[sl][:, 0:W], st.pb[sl][:, 0:W],
                1.0, mask[:, 0:W], ALU.mult, ALU.mult,
                accum_out=st.sacc[:, (sl % 2) + 2 * (pp % 2):(sl % 2) + 2 * (pp % 2) + 1]),
                [("act", exp_cnt[pp])])
        s0 = 2 * (pp % 2)
        ctx.op("dve", lambda e, s0=s0: e.reciprocal(
            st.sacc[:, 4 + s0:6 + s0], st.sacc[:, s0:s0 + 2]), [("dve", None)])
        for par in (0, 1):
            sl = 2 * (pp % 4) + par
            ts_cnt[pp] = ctx.op("dve", lambda e, sl=sl, s0=s0, par=par, W=W: e.tensor_scalar(
                st.pb[sl][:, 0:W], st.pmb[sl][:, 0:W],
                st.sacc[:, 4 + s0 + par:5 + s0 + par], None, ALU.mult),
                [("dve", None)])

    def stage2a(pp):
        W, wcol, qcol, mask, c = chp[pp // 8]
        vslot = pp % 4
        vbank = 6 + pp % 2
        vsrc = wv(st.qkvb, 16 + pp % 8, 528)
        ctx.mm(vbank, lambda e, vbank=vbank, vsrc=vsrc, wcol=wcol: e.transpose(
            st.Pb[vbank][:, 256:384], vsrc[:, wcol:wcol + 128], st.identb_sb[:]),
            qkvdep)
        ctx.mm(vbank, lambda e, vbank=vbank, vsrc=vsrc, wcol=wcol, W=W: e.transpose(
            st.Pb[vbank][0:W - 128, 384:512], vsrc[:, wcol + 128:wcol + W],
            st.identb_sb[:]))
        vw_evac[pp] = ctx.rd("act", vbank, lambda e, vbank=vbank, vslot=vslot: e.activation(
            wv(st.vwin, vslot, 256), st.Pb[vbank][:, 256:512], AF.Copy))
        for par in (0, 1):
            sl = 2 * (pp % 4) + par
            ptb = st.Pb[2 + par]
            ctx.mm(2 + par, lambda e, ptb=ptb, sl=sl: e.transpose(
                ptb[:, 0:128], st.pb[sl][:, 0:128], st.identb_sb[:]),
                [("dve", ts_cnt[pp])] if par == 0 else [])
            ctx.mm(2 + par, lambda e, ptb=ptb, sl=sl, W=W: e.transpose(
                ptb[0:W - 128, 128:256], st.pb[sl][:, 128:W], st.identb_sb[:]))
            pb_reader[sl] = ctx.n["pe"]
        for par in (0, 1):
            pt_evac[pp] = ctx.rd("dve", 2 + par, lambda e, par=par, pp=pp:
                e.tensor_copy(wv(st.ptsb, 2 * (pp % 3) + par, 256),
                              st.Pb[2 + par][:, 0:256]))

    def stage2b(pp):
        W, wcol, qcol, mask, c = chp[pp // 8]
        vslot = pp % 4
        for par in (0, 1):
            ob = 6 + par
            va = wv(st.vwin, vslot, 256, (64 * par, 64 * par + 64))
            vb = wv(st.vwin, vslot, 256, (128 + 64 * par, 128 + 64 * par + 64))
            ctx.mm(ob, lambda e, par=par, va=va, ob=ob, pp=pp: e.matmul(
                st.P[ob][0:64, 0:128], va,
                wv(st.ptsb, 2 * (pp % 3) + par, 256, (0, 128)),
                start=True, stop=False),
                [("dve", pt_evac[pp]), ("act", vw_evac[pp])] if par == 0 else [])
            ctx.mm(ob, lambda e, par=par, vb=vb, ob=ob, W=W, pp=pp: e.matmul(
                st.P[ob][0:64, 0:128], vb[0:W - 128, :],
                wv(st.ptsb, 2 * (pp % 3) + par, 256, (128, 256))[0:W - 128, :],
                start=False, stop=True))
        for par in (0, 1):
            ob = 6 + par
            ctx.rd("act", ob, lambda e, par=par, ob=ob, pp=pp, c=c: e.activation(
                wv(st.otsb, pp % 8, 512, (128 * c, 128 * c + 128))[64 * par:64 * par + 64, :],
                st.P[ob][0:64, 0:128], AF.Copy))

    D1, D2A, D2B = 3, 6, 9
    for step in range(32 + D2B):
        if step >= D2B:
            stage2b(step - D2B)
        if D2A <= step < 32 + D2A:
            stage2a(step - D2A)
        if D1 <= step < 32 + D1:
            stage1(step - D1)
        if step < 32:
            s_front(2 * step)
            s_front(2 * step + 1)


def ln_stats(ctx, src, src_cnt, banks, sq):
    st = ctx.S
    b1, b2 = banks
    sqc = {}
    mm2c = {}
    for m in range(8):
        sdep = [("dve", src_cnt[m] if src_cnt else None)]
        sqc[m] = ctx.op("act", lambda e, m=m: e.activation(
            sq[m % 2], src[:, m, :], AF.Square),
            sdep + ([("pe", None)] if m < 2 else [("pe", mm2c[m - 2])]))
        ctx.mm(b1, lambda e, m=m: e.matmul(
            st.P[b1][0:1, 0:512], st.onescol_sb[:], src[:, m, :],
            start=m == 0, stop=m == 7), sdep)
        mm2c[m] = ctx.mm(b2, lambda e, m=m: e.matmul(
            st.P[b2][0:1, 0:512], st.onescol_sb[:], sq[m % 2],
            start=m == 0, stop=m == 7), [("act", sqc[m])])
    return mm2c


def ln_rows(ctx, rows, t0, banks, mm2c7=None):
    st = ctx.S
    b1, b2 = banks
    mrow = rows[0:1, 0:512]
    rrow = rows[0:1, 512:1024]
    ctx.rd("dve", b1, lambda e: e.tensor_scalar(mrow, st.P[b1][0:1, 0:512],
                                                1.0 / D, None, ALU.mult))
    ctx.rd("dve", b2, lambda e: e.tensor_scalar(t0, st.P[b2][0:1, 0:512],
                                                1.0 / D, None, ALU.mult),
           [("pe", mm2c7)] if mm2c7 else [])
    ctx.op("dve", lambda e: e.tensor_tensor(rrow, mrow, mrow, ALU.mult),
           [("dve", None)])
    ctx.op("dve", lambda e: e.tensor_tensor(t0, t0, rrow, ALU.subtract),
           [("dve", None)])
    lnc = ctx.op("act", lambda e: e.activation(rrow, t0, AF.Ln,
        bias=st.srow[0:1, SL_EPS:SL_EPS + 1]), [("dve", None)])
    rse = ctx.op("act", lambda e: e.activation(rrow, rrow, AF.Exp, scale=-0.5),
                 [("act", lnc)])
    return rse


def ln_bcast(ctx, rows, banks, rse):
    st = ctx.S
    b1, b2 = banks
    ctx.mm(b1, lambda e: e.matmul(st.P[b1][:, 0:512], st.onesrow_sb[:],
                                  rows[0:1, 0:512], start=True, stop=True),
           [("dve", None)])
    ctx.mm(b2, lambda e: e.matmul(st.P[b2][:, 0:512], st.onesrow_sb[:],
                                  rows[0:1, 512:1024], start=True, stop=True),
           [("act", rse)])


def ln_norm(ctx, src, dst, tsc, banks, rse):
    st = ctx.S
    b1, b2 = banks
    norm_cnt = {}
    for m in range(8):
        ctx.rd("dve", b1, lambda e, m=m: e.tensor_tensor(
            tsc[:], src[:, m, :], st.P[b1][:, 0:512], ALU.subtract),
            [("act", rse), ("dve", None)] if m == 0 else [("dve", None)])
        norm_cnt[m] = ctx.rd("dve", b2, lambda e, m=m: e.scalar_tensor_tensor(
            dst[:, m, :], tsc[:], 1.0, st.P[b2][:, 0:512],
            ALU.mult, ALU.mult), [("dve", None)])
    return norm_cnt


def ln_stats_rows_a(ctx, proj_cnt):
    st = ctx.S
    src = st.f2[:].rearrange("p (c n) -> p c n", c=8)
    mm2c = ln_stats(ctx, src, proj_cnt, (6, 7), st.sq_a)
    return ln_rows(ctx, st.rows_a, st.tsc_a[0:1, 0:512], (6, 7), mm2c[7])


def ln_finish_a(ctx, b, q, rse):
    st = ctx.S
    b1, b2 = 6, 7
    src = st.f2[:].rearrange("p (c n) -> p c n", c=8)
    tsc = st.tsc_a
    mrow = st.rows_a[0:1, 0:512]
    rrow = st.rows_a[0:1, 512:1024]
    ctx.mm(b1, lambda e: e.matmul(st.P[b1][:, 0:512], st.onesrow_sb[:], mrow,
                                  start=True, stop=True))
    ctx.mm(b2, lambda e: e.matmul(st.P[b2][:, 0:512], st.onesrow_sb[:], rrow,
                                  start=True, stop=True), [("act", rse)])
    x1v = st.x1roll.rearrange("p (c n) -> p c n", c=2)
    spill_cnt = {}
    for m in range(8):
        ctx.rd("dve", b1, lambda e, m=m: e.tensor_tensor(
            tsc[:], src[:, m, :], st.P[b1][:, 0:512], ALU.subtract),
            [("dve", None)])
        acc = wv(st.gcparts, m, 16, (4 * b + q, 4 * b + q + 1))
        deps = ([("sp_dma", spill_cnt[m - 2])] if m >= 2
                else ([("sp_dma", st.spill_last[m % 2])]
                      if m % 2 in st.spill_last else []))
        ctx.rd("dve", b2, lambda e, m=m, acc=acc: e.scalar_tensor_tensor(
            x1v[:, m % 2, :], tsc[:], 1.0, st.P[b2][:, 0:512],
            ALU.mult, ALU.mult, accum_out=acc), deps + [("dve", None)])
        ctx.q_serialize("sp_dma")
        spill_cnt[m] = ctx.op("sp_dma", lambda e, m=m: e.dma_start(
            out=st.x1sp[b, m, :, 512 * q:512 * q + 512], in_=x1v[:, m % 2, :]),
            deps=[("dve", None)])
        st.spill_last[m % 2] = spill_cnt[m]
    st.spill_q[(b, q)] = spill_cnt[7]


def phase_b(ctx):
    st = ctx.S
    gp = st.gcparts[:].rearrange("p (c n) -> p c n", c=8)
    gs = st.gcsum[:].rearrange("p (b n) -> p b n", b=4)
    gb = st.gcbf[:].rearrange("p (b n) -> p b n", b=4)
    for b in range(st.n_b):
        dep = [("dve", None), ("sp_dma", st.c_sp_bias)] if b == 0 else []
        ctx.op("dve", lambda e, b=b: e.tensor_tensor(
            gs[:, b, :], gp[:, :, 4 * b + 0], gp[:, :, 4 * b + 1], ALU.add),
            dep)
        ctx.op("dve", lambda e, b=b: e.tensor_tensor(
            gs[:, b, :], gs[:, b, :], gp[:, :, 4 * b + 2], ALU.add),
            [("dve", None)])
        ctx.op("dve", lambda e, b=b: e.tensor_tensor(
            gs[:, b, :], gs[:, b, :], gp[:, :, 4 * b + 3], ALU.add),
            [("dve", None)])
        ctx.op("dve", lambda e, b=b: e.tensor_scalar(
            gs[:, b, :], gs[:, b, :], 1.0 / SEQ, None, ALU.mult),
            [("dve", None)])
        ctx.op("dve", lambda e, b=b: e.tensor_copy(gb[:, b, :], gs[:, b, :]),
               [("dve", None)])
    for m in range(4):
        for b in range(st.n_b):
            for k in range(8):
                lhs = wv(st.wgc1_sb, k, 512, (128 * m, 128 * m + 128))
                ctx.mm(b, lambda e, lhs=lhs, k=k, m=m, b=b: e.matmul(
                    st.P[b][:, m:m + 1], lhs, gb[:, b, k:k + 1],
                    start=k == 0, stop=k == 7),
                    [("dve", None), ("pl_dma", st.c_pl_wall)]
                    if (m == 0 and k == 0) else [])
        for b in range(st.n_b):
            ctx.rd("act", b, lambda e, m=m, b=b: e.activation(
                st.g1bf[:, 4 * b + m:4 * b + m + 1], st.P[b][:, m:m + 1],
                AF.Gelu, bias=st.bgc1_sb[:, m:m + 1]))
    for m in range(2):
        for b in range(st.n_b):
            for k in range(4):
                lhs = wv(st.wgc2_sb, k, 256, (128 * m, 128 * m + 128))
                ctx.mm(b, lambda e, lhs=lhs, k=k, m=m, b=b: e.matmul(
                    st.P[b][:, 128 + m:129 + m], lhs,
                    st.g1bf[:, 4 * b + k:4 * b + k + 1],
                    start=k == 0, stop=k == 3),
                    [("act", None)] if (m == 0 and b == 0) else [])
        for b in range(st.n_b):
            ctx.rd("act", b, lambda e, m=m, b=b: e.activation(
                st.g2bf[:, 2 * b + m:2 * b + m + 1],
                st.P[b][:, 128 + m:129 + m],
                AF.Gelu, bias=st.bgc2_sb[:, m:m + 1]))
    for b in range(st.n_b):
        for k in range(2):
            ctx.mm(b, lambda e, k=k, b=b: e.matmul(
                st.P[b][0:1, 256:257], st.wgc3_sb[:, k:k + 1],
                st.g2bf[:, 2 * b + k:2 * b + k + 1],
                start=k == 0, stop=k == 1),
                [("act", None)] if (k == 0 and b == 0) else [])
    for b in range(st.n_b):
        ctx.rd("act", b, lambda e, b=b: e.activation(
            st.srow[0:1, 20 + b:21 + b], st.P[b][0:1, 256:257], AF.Tanh,
            scale=0.5, bias=st.srow[0:1, SL_HB3:SL_HB3 + 1]))
    for b in range(st.n_b):
        ctx.op("dve", lambda e, b=b: e.tensor_scalar(
            st.srow[0:1, SL_CB2 + b:SL_CB2 + b + 1], st.srow[0:1, 20 + b:21 + b],
            st.srow[0:1, SL_E7:SL_E7 + 1], 0.1, ALU.mult, ALU.add),
            [("act", None)] + ([("dve", None)] if b == 0 else []))


TW_SRC = [4, 5, 6, 7, 0, 1, 2, 3]
TW_SGN = [1.0, 1.0, -1.0, -1.0, -1.0, -1.0, 1.0, 1.0]


def c_views(st, side):
    if side == 0:
        return dict(
            x1=st.xtok[:, 2048:6144].rearrange("p (c n) -> p c n", c=8),
            s0=st.f1[:].rearrange("p (c n) -> p c n", c=8),
            s1=st.f2[:].rearrange("p (c n) -> p c n", c=8),
            obv=st.qkvb[:, 7168:11264].rearrange("p (c n) -> p c n", c=8),
            tsc=st.tsc_c, rows=st.rows_c)
    return dict(
        x1=st.x1B.rearrange("p (c n) -> p c n", c=8),
        s0=st.sb0.rearrange("p (c n) -> p c n", c=8),
        s1=st.sb1.rearrange("p (c n) -> p c n", c=8),
        obv=st.obvB.rearrange("p (c n) -> p c n", c=8),
        tsc=st.tscB, rows=st.rowsB)


def emit_c_reload(ctx, b, q, vw_):
    st = ctx.S
    ctx.q_serialize("sp_dma")
    for m in range(8):
        ctx.op("sp_dma", lambda e, m=m: e.dma_start(
            out=vw_["x1"][:, m, :], in_=st.x1sp[b, m, :, 512 * q:512 * q + 512]),
            deps=[("dve", None), ("pe", None), ("act", None), ("pl", None)])
    st.c_rld[(b, q)] = ctx.n["sp_dma"]


def emit_c_obv_load(ctx, b, q, vw_):
    st = ctx.S
    ctx.q_serialize("pl_dma")
    for m in range(8):
        ctx.op("pl_dma", lambda e, m=m: e.dma_start(
            out=vw_["obv"][:, m, :], in_=st.x1sp[b, m, :, 512 * q:512 * q + 512]),
            deps=[("pe", None), ("act", None), ("sp_dma", st.spill_q[(b, q)])]
            if m == 0 else [])
    cnt = ctx.n["pl_dma"]
    return {m: ("pl_dma", cnt) for m in range(8)}


def c_chain(ctx, b, vw_, obv_cnt, cb):
    st = ctx.S
    h1v = st.qkvb[:, 0:4096].rearrange("p (c n) -> p c n", c=8)
    h2v = st.qkvb[:, 4096:6144].rearrange("p (c n) -> p c n", c=4)
    h3v = st.qkvb[:, 6144:7168].rearrange("p (c n) -> p c n", c=2)
    obv = vw_["obv"]
    throw_ = vw_["rows"][0:1, 0:512]
    crow = vw_["rows"][0:1, 512:1024]
    h1e = {}
    for half in range(2):
        for k in range(8):
            for mi in range(4):
                m = 4 * half + mi
                ctx.mm(mi, lambda e, m=m, k=k, mi=mi: e.matmul(
                    st.P[mi][:, 0:512],
                    wv(st.wcn1_sb, k, 1024, (128 * m, 128 * m + 128))
                    if False else st.wcn1v[k][:, 128 * m:128 * m + 128],
                    obv[:, k, :], start=k == 0, stop=k == 7),
                    [obv_cnt[k]] if mi == 0 and half == 0 else [])
        for mi in range(4):
            m = 4 * half + mi
            h1e[m] = ctx.rd("act", mi, lambda e, m=m, mi=mi: e.activation(
                h1v[:, m, :], st.P[mi][:, 0:512], AF.Gelu,
                bias=st.bcn1_sb[:, m:m + 1]))
    for k in range(8):
        for m in range(4):
            ctx.mm(m, lambda e, m=m, k=k: e.matmul(
                st.P[m][:, 0:512], st.wcn2v[k][:, 128 * m:128 * m + 128],
                h1v[:, k, :], start=k == 0, stop=k == 7),
                [("act", h1e[k])] if m == 0 else [])
    h2e = {}
    for m in range(4):
        h2e[m] = ctx.rd("act", m, lambda e, m=m: e.activation(
            h2v[:, m, :], st.P[m][:, 0:512], AF.Gelu, bias=st.bcn2_sb[:, m:m + 1]))
    for k in range(4):
        for m in range(2):
            ctx.mm(m, lambda e, m=m, k=k: e.matmul(
                st.P[m][:, 0:512], st.wcn3v[k][:, 128 * m:128 * m + 128],
                h2v[:, k, :], start=k == 0, stop=k == 3),
                [("act", h2e[k])] if m == 0 else [])
    h3e = {}
    for m in range(2):
        h3e[m] = ctx.rd("act", m, lambda e, m=m: e.activation(
            h3v[:, m, :], st.P[m][:, 0:512], AF.Gelu, bias=st.bcn3_sb[:, m:m + 1]))
    for k in range(2):
        ctx.mm(4, lambda e, k=k: e.matmul(
            st.P[4][0:1, 0:512], st.wcn4_sb[:, k:k + 1], h3v[:, k, :],
            start=k == 0, stop=k == 1), [("act", h3e[k])])
    ctx.rd("act", 4, lambda e: e.activation(
        throw_, st.P[4][0:1, 0:512], AF.Tanh, scale=0.5,
        bias=st.srow[0:1, SL_HB4:SL_HB4 + 1]))
    ctx.op("dve", lambda e: e.tensor_scalar(
        crow, throw_, st.srow[0:1, SL_A2:SL_A2 + 1],
        st.srow[0:1, SL_CB2 + b:SL_CB2 + b + 1], ALU.mult, ALU.add),
        [("act", None), ("dve", None)])
    ctx.mm(cb, lambda e: e.matmul(st.P[cb][:, 0:512], st.onesrow_sb[:], crow,
                                  start=True, stop=True), [("dve", None)])


def c_twist(ctx, vw_, cyc, curv, nxtv, obv_cnt, cb, extra_dep=()):
    st = ctx.S
    tsc = vw_["tsc"]
    for m in range(8):
        d = list(extra_dep) if m == 0 else []
        ctx.rd("dve", cb, lambda e, m=m: e.scalar_tensor_tensor(
            tsc[:], st.P[cb][:, 0:512], TW_SGN[m], curv[:, TW_SRC[m], :],
            ALU.mult, ALU.mult), d + [("dve", None)])
        adv = ctx.op("dve", lambda e, m=m: e.tensor_tensor(
            nxtv[:, m, :], curv[:, m, :], tsc[:], ALU.add), [("dve", None)])
        if cyc < 2:
            obv_cnt[m] = ("pl", ctx.op("pl", lambda e, m=m: e.tensor_copy(
                vw_["obv"][:, m, :], nxtv[:, m, :]), [("dve", adv)]))


def c_out(ctx, b, q, srcv, norm_cnt, side):
    st = ctx.S
    ctx.q_serialize("act_dma")
    for m in range(8):
        ctx.op("act_dma", lambda e, m=m: e.dma_start(
            out=st.out[b, m, :, 512 * q:512 * q + 512], in_=srcv[:, m, :]),
            deps=[("dve", norm_cnt[m])])
    st.out_tr[side] = ("act_dma", ctx.n["act_dma"])



def phase_c_pair(ctx, b, qa):
    st = ctx.S
    vwA = c_views(st, 0)
    vwB = c_views(st, 1)
    first_dep = [("pe", None), ("dve", None), ("act", None)]

    cA = st.c_pre.pop((0, b, qa), None)
    if cA is None:
        if (b, qa) not in st.c_rld:
            emit_c_reload(ctx, b, qa, vwA)
        cA = st.c_obv_pre.pop((b, qa), None) or             emit_c_obv_load(ctx, b, qa, vwA)
        c_chain(ctx, b, vwA, cA, 5)
    cB = st.c_pre.pop((1, b, qa + 1), None)
    if cB is None:
        if (b, qa + 1) not in st.c_rld:
            emit_c_reload(ctx, b, qa + 1, vwB)
        cB = st.c_obv_pre.pop((b, qa + 1), None) or             emit_c_obv_load(ctx, b, qa + 1, vwB)
        c_chain(ctx, b, vwB, cB, 7)
    nb_, nqa = (b, qa + 2) if qa + 2 < NQ else (b + 1, 0)
    have_next = nb_ < st.n_b
    otlA = ([st.out_tr[0]] if st.out_tr[0] else [("pe", None)]) + \
        [("sp_dma", st.c_rld[(b, qa)])]
    otlB = ([st.out_tr[1]] if st.out_tr[1] else [("pe", None)]) + \
        [("sp_dma", st.c_rld[(b, qa + 1)])]
    c_twist(ctx, vwA, 0, vwA["x1"], vwA["s0"], cA, 5, extra_dep=otlA)
    c_twist(ctx, vwB, 0, vwB["x1"], vwB["s0"], cB, 7, extra_dep=otlB)
    c_chain(ctx, b, vwA, cA, 5)
    c_twist(ctx, vwA, 1, vwA["s0"], vwA["s1"], cA, 5)
    c_chain(ctx, b, vwB, cB, 7)
    c_twist(ctx, vwB, 1, vwB["s0"], vwB["s1"], cB, 7)
    c_chain(ctx, b, vwA, cA, 5)
    c_twist(ctx, vwA, 2, vwA["s1"], vwA["s0"], cA, 5)
    if have_next:
        cNA = emit_c_obv_load(ctx, nb_, nqa, vwA)
    c_chain(ctx, b, vwB, cB, 7)
    c_twist(ctx, vwB, 2, vwB["s1"], vwB["s0"], cB, 7)
    if have_next:
        cNB = emit_c_obv_load(ctx, nb_, nqa + 1, vwB)
    resA = {}
    resB = {}
    for m in range(8):
        resA[m] = ctx.op("dve", lambda e, m=m: e.tensor_tensor(
            vwA["s1"][:, m, :], vwA["x1"][:, m, :], vwA["s0"][:, m, :], ALU.add),
            [("dve", None)] if m == 0 else [])
    for m in range(8):
        resB[m] = ctx.op("dve", lambda e, m=m: e.tensor_tensor(
            vwB["s1"][:, m, :], vwB["x1"][:, m, :], vwB["s0"][:, m, :], ALU.add),
            [("dve", None)] if m == 0 else [])
    if have_next:
        emit_c_reload(ctx, nb_, nqa, vwA)
        emit_c_reload(ctx, nb_, nqa + 1, vwB)
        c_chain(ctx, nb_, vwA, cNA, 5)
        c_chain(ctx, nb_, vwB, cNB, 7)
        st.c_pre[(0, nb_, nqa)] = cNA
        st.c_pre[(1, nb_, nqa + 1)] = cNB
    ln_stats(ctx, vwA["s1"], resA, (0, 1), st.sq_cA)
    ln_stats(ctx, vwB["s1"], resB, (2, 3), st.sq_cB)
    rseA = ln_rows(ctx, vwA["rows"], vwA["tsc"][0:1, 0:512], (0, 1))
    rseB = ln_rows(ctx, vwB["rows"], vwB["tsc"][0:1, 0:512], (2, 3))
    ln_bcast(ctx, vwA["rows"], (0, 1), rseA)
    ln_bcast(ctx, vwB["rows"], (2, 3), rseB)
    nA = ln_norm(ctx, vwA["s1"], vwA["s0"], vwA["tsc"], (0, 1), rseA)
    c_out(ctx, b, qa, vwA["s0"], nA, 0)
    nB = ln_norm(ctx, vwB["s1"], vwB["s0"], vwB["tsc"], (2, 3), rseB)
    c_out(ctx, b, qa + 1, vwB["s0"], nB, 1)


def build_nc(n_b=NB):
    nc = bass.Bass("TRN2", target_bir_lowering=False)
    st = build_state(nc, n_b)
    blk = st.block
    for name, attr in (("sp", "sync"), ("pl", "gpsimd"), ("dve", "vector"),
                       ("act", "scalar"), ("pe", "tensor")):
        def make(name):
            def run(eng):
                program(Ctx(st, name, eng))
            return run
        getattr(blk, attr)(make(name))
    st.ex.close()
    return nc


def kernel(L0_attn_x, L0_attn_Wqkv, L0_attn_Wproj, L0_attn_bproj,
           L0_ffn_cn_w1, L0_ffn_cn_b1, L0_ffn_cn_w2, L0_ffn_cn_b2,
           L0_ffn_cn_w3, L0_ffn_cn_b3, L0_ffn_cn_w4, L0_ffn_cn_b4,
           L0_ffn_gc_w1, L0_ffn_gc_b1, L0_ffn_gc_w2, L0_ffn_gc_b2,
           L0_ffn_gc_w3, L0_ffn_gc_b3, L0_ffn_adaptive_range):
    x = np.ascontiguousarray(np.asarray(L0_attn_x, dtype=np.float32))
    shared = {
        "wqkv": np.ascontiguousarray(L0_attn_Wqkv, dtype=np.float32),
        "wproj": np.ascontiguousarray(L0_attn_Wproj, dtype=np.float32),
        "bproj": np.ascontiguousarray(L0_attn_bproj, dtype=np.float32),
        "cn_w1": np.ascontiguousarray(L0_ffn_cn_w1, dtype=np.float32),
        "cn_b1": np.ascontiguousarray(L0_ffn_cn_b1, dtype=np.float32),
        "cn_w2": np.ascontiguousarray(L0_ffn_cn_w2, dtype=np.float32),
        "cn_b2": np.ascontiguousarray(L0_ffn_cn_b2, dtype=np.float32),
        "cn_w3": np.ascontiguousarray(L0_ffn_cn_w3, dtype=np.float32),
        "cn_b3": np.ascontiguousarray(L0_ffn_cn_b3, dtype=np.float32),
        "cn_w4": np.ascontiguousarray(L0_ffn_cn_w4, dtype=np.float32),
        "cn_b4": np.ascontiguousarray(L0_ffn_cn_b4, dtype=np.float32),
        "gc_w1": np.ascontiguousarray(L0_ffn_gc_w1, dtype=np.float32),
        "gc_b1": np.ascontiguousarray(L0_ffn_gc_b1, dtype=np.float32),
        "gc_w2": np.ascontiguousarray(L0_ffn_gc_w2, dtype=np.float32),
        "gc_b2": np.ascontiguousarray(L0_ffn_gc_b2, dtype=np.float32),
        "gc_w3": np.ascontiguousarray(L0_ffn_gc_w3, dtype=np.float32),
        "gc_b3": np.ascontiguousarray(L0_ffn_gc_b3, dtype=np.float32),
        "ar": np.ascontiguousarray(L0_ffn_adaptive_range, dtype=np.float32),
        **_const_inputs(),
    }
    nc = build_nc(NB)
    in_maps = [dict(shared, x=np.ascontiguousarray(x[4 * c:4 * c + 4]))
               for c in range(N_CORE)]
    res = run_bass_kernel_spmd(nc, in_maps, core_ids=list(range(N_CORE)))
    arr = np.concatenate([r["out"] for r in res.results], axis=0)
    return np.ascontiguousarray(
        arr.transpose(0, 3, 1, 2).reshape(32 * N_CORE // 8, SEQ, D))


# version 44 (final)
# speedup vs baseline: 1.0097x; 1.0072x over previous
import numpy as np
import ml_dtypes
from contextlib import ExitStack

import concourse.bass as bass
import concourse.mybir as mybir
import concourse.bass_utils as _bu
from concourse.bass_utils import run_bass_kernel_spmd

_orig_run_command = _bu.run_command


def _patched_run_command(argv, **kwargs):
    argv = ["--enable-ldw-opt=true" if a == "--enable-ldw-opt=false" else a
            for a in argv]
    return _orig_run_command(argv, **kwargs)


if _bu.run_command is not _patched_run_command:
    _bu.run_command = _patched_run_command

F32 = mybir.dt.float32
BF16 = mybir.dt.bfloat16
AF = mybir.ActivationFunctionType
ALU = mybir.AluOpType

N_CORE = 8
NB = 4
SEQ = 2048
D = 1024
NQ = 4
NH = 16
EPS = 1e-5

ENG_OF = {"sp_dma": "sp", "pl_dma": "pl", "act_dma": "act"}


def _const_inputs():
    p = np.arange(128)[:, None]
    w = np.arange(144)[None, :]
    return {
        "ident_f": np.eye(128, dtype=np.float32),
        "ident_b": np.eye(128, dtype=ml_dtypes.bfloat16),
        "ones_row": np.ones((1, 128), dtype=np.float32),
        "ones_col": np.ones((128, 1), dtype=np.float32),
        "mask0": (np.abs(p - w) <= 8).astype(ml_dtypes.bfloat16),
        "maskM": ((w >= p) & (w <= p + 16)).astype(ml_dtypes.bfloat16),
    }


class S:
    pass


class Ctx:

    def __init__(self, st, cur, eng):
        self.S = st
        self.cur = cur
        self.e = eng
        self.n = {k: 0 for k in ("pe", "dve", "act", "sp", "pl", "sp_dma", "pl_dma", "act_dma")}
        self.waited = {}
        self.bank_w = {}
        self.bank_r = {}

    def _wait(self, key, cnt):
        if ENG_OF.get(key, key) == self.cur:
            return
        val = cnt * (16 if key.endswith("_dma") else 1)
        if val <= 0 or self.waited.get(key, 0) >= val:
            return
        self.waited[key] = val
        self.e.wait_ge(self.S.sems[key], val)

    def raw_wait(self, key, cnt=None):
        c = self.n[key] if cnt is None else cnt
        val = c * (16 if key.endswith("_dma") else 1)
        if val <= 0 or self.waited.get(key, 0) >= val:
            return
        self.waited[key] = val
        self.e.wait_ge(self.S.sems[key], val)

    def q_serialize(self, qkey):
        if ENG_OF[qkey] == self.cur:
            self.raw_wait(qkey, self.n[qkey])

    def op(self, key, fn, deps=()):
        eng = ENG_OF.get(key, key)
        if eng == self.cur:
            for (tk, tc) in deps:
                cnt = self.n[tk] if tc is None else tc
                if ENG_OF.get(tk, tk) == self.cur:
                    self.raw_wait(tk, cnt)
                else:
                    self._wait(tk, cnt)
        self.n[key] += 1
        if eng == self.cur:
            fn(self.e).then_inc(self.S.sems[key], 16 if key.endswith("_dma") else 1)
        return self.n[key]

    def mm(self, bank, fn, deps=()):
        d = list(deps)
        rs = self.bank_r.get(bank)
        if rs:
            d += list(rs.items())
        n = self.op("pe", fn, d)
        self.bank_r[bank] = {}
        self.bank_w[bank] = n
        return n

    def rd(self, key, bank, fn, deps=()):
        d = list(deps) + [("pe", self.bank_w.get(bank, 0))]
        n = self.op(key, fn, d)
        self.bank_r.setdefault(bank, {})[key] = n
        return n


def wv(t, chunk, width, sub=None):
    lo = chunk * width + (sub[0] if sub else 0)
    hi = chunk * width + (sub[1] if sub else width)
    return t[:, lo:hi]


def build_state(nc, n_b):
    st = S()
    st.nc = nc
    st.n_b = n_b

    def din(name, shape, dtype=F32):
        return nc.dram_tensor(name, list(shape), dtype, kind="ExternalInput").ap()

    st.x = din("x", [n_b, SEQ, D])
    st.wqkv = din("wqkv", [D, 3 * D])
    st.wproj = din("wproj", [D, D])
    st.bproj = din("bproj", [D])
    st.cn_w1 = din("cn_w1", [D, D]); st.cn_b1 = din("cn_b1", [D])
    st.cn_w2 = din("cn_w2", [D, 512]); st.cn_b2 = din("cn_b2", [512])
    st.cn_w3 = din("cn_w3", [512, 256]); st.cn_b3 = din("cn_b3", [256])
    st.cn_w4 = din("cn_w4", [256, 1]); st.cn_b4 = din("cn_b4", [1])
    st.gc_w1 = din("gc_w1", [D, 512]); st.gc_b1 = din("gc_b1", [512])
    st.gc_w2 = din("gc_w2", [512, 256]); st.gc_b2 = din("gc_b2", [256])
    st.gc_w3 = din("gc_w3", [256, 1]); st.gc_b3 = din("gc_b3", [1])
    st.ar = din("ar", [1])
    st.ident_f = din("ident_f", [128, 128])
    st.ident_b = din("ident_b", [128, 128], BF16)
    st.ones_row = din("ones_row", [1, 128])
    st.ones_col = din("ones_col", [128, 1])
    st.mask0 = din("mask0", [128, 144], BF16)
    st.maskM = din("maskM", [128, 144], BF16)

    st.out = nc.dram_tensor("out", [n_b, 8, 128, SEQ], F32,
                        kind="ExternalOutput").ap()
    st.x1sp = nc.dram_tensor("x1sp", [n_b, 8, 128, SEQ], F32, kind="Internal").ap()

    st.ex = ExitStack()
    e = st.ex.enter_context

    def sb(name, shape, dtype):
        return e(nc.sbuf_tensor(name, list(shape), dtype))

    st.wqkv_sb = sb("wqkv_sb", [128, 8 * 3072], BF16)
    st.wproj_sb = sb("wproj_sb", [128, 8 * 1024], BF16)
    st.wcn1_sb = sb("wcn1_sb", [128, 8 * 1024], BF16)
    st.wcn2_sb = sb("wcn2_sb", [128, 8 * 512], BF16)
    st.wcn3_sb = sb("wcn3_sb", [128, 4 * 256], BF16)
    st.wcn4_sb = sb("wcn4_sb", [128, 2], BF16)
    st.wgc1_sb = sb("wgc1_sb", [128, 8 * 512], BF16)
    st.wgc2_sb = sb("wgc2_sb", [128, 4 * 256], BF16)
    st.wgc3_sb = sb("wgc3_sb", [128, 2], BF16)
    st.bproj_sb = sb("bproj_sb", [128, 8], F32)
    st.bcn1_sb = sb("bcn1_sb", [128, 8], F32)
    st.bcn2_sb = sb("bcn2_sb", [128, 4], F32)
    st.bcn3_sb = sb("bcn3_sb", [128, 2], F32)
    st.bgc1_sb = sb("bgc1_sb", [128, 4], F32)
    st.bgc2_sb = sb("bgc2_sb", [128, 2], F32)
    st.srow = sb("srow", [1, 64], F32)
    st.identf_sb = sb("identf_sb", [128, 128], F32)
    st.identb_sb = sb("identb_sb", [128, 128], BF16)
    st.onesrow_sb = sb("onesrow_sb", [1, 128], F32)
    st.onescol_sb = sb("onescol_sb", [128, 1], F32)
    st.mask0_sb = sb("mask0_sb", [128, 144], BF16)
    st.maskM_sb = sb("maskM_sb", [128, 144], BF16)

    st.xtok = sb("xtok", [128, 6 * 1024], F32)
    st.xtb = sb("xtb", [128, 8 * 528], BF16)
    st.qkvb = sb("qkvb", [128, 24 * 528], BF16)
    st.vwin = sb("vwin", [128, 4 * 256], BF16)
    st.ptsb = sb("ptsb", [128, 6 * 256], BF16)
    st.sacc = sb("sacc", [128, 8], F32)
    st.otsb = sb("otsb", [128, 8 * 512], BF16)
    st.f2 = sb("f2", [128, 8 * 512], F32)
    st.f1 = sb("f1", [128, 8 * 512], F32)
    st.gcparts = sb("gcparts", [128, 8 * 16], F32)
    st.gcsum = sb("gcsum", [128, 32], F32)
    st.gcbf = sb("gcbf", [128, 32], BF16)
    st.g1bf = sb("g1bf", [128, 16], BF16)
    st.g2bf = sb("g2bf", [128, 8], BF16)

    st.P = [e(nc.psum_tensor(f"P{i}", [128, 512], F32)) for i in range(8)]
    st.wcn1v = [wv(st.wcn1_sb, k, 1024) for k in range(8)]
    st.wcn2v = [wv(st.wcn2_sb, k, 512) for k in range(8)]
    st.wcn3v = [wv(st.wcn3_sb, k, 256) for k in range(4)]
    st.x1roll = st.f1[:, 1024:2048]
    st.tsc_a = st.f1[:, 2048:2560]
    st.rows_a = st.f1[0:1, 2560:3584]
    st.sq_a = [st.f1[:, 3584:4096], st.f1[:, 2048:2560]]
    xtbf32 = st.xtb[:].bitcast(F32)
    st.tsc_c = xtbf32[:, 0:512]
    st.sq_cB = [xtbf32[:, 512:1024], xtbf32[:, 1024:1536]]
    otsbf32 = st.otsb[:].bitcast(F32)
    st.rows_c = otsbf32[0:1, 1024:2048]
    st.sq_cA = [otsbf32[:, 0:512], otsbf32[:, 512:1024]]
    st.pb = [st.xtb[:, 1024 + 160 * i: 1024 + 160 * (i + 1)] for i in range(8)]
    st.pmb = [st.xtb[:, 2304 + 160 * i: 2304 + 160 * (i + 1)] for i in range(8)]
    wqf = st.wqkv_sb[:].bitcast(F32)
    st.sb0 = wqf[:, 0:4096]
    st.sb1 = wqf[:, 4096:8192]
    st.x1B = wqf[:, 8192:12288]
    st.obvB = st.wproj_sb[:, 0:4096]
    wpf = st.wproj_sb[:].bitcast(F32)
    st.tscB = wpf[:, 2048:2560]
    st.rowsB = wpf[0:1, 2560:3584]
    st.Pb = [st.P[i][:].bitcast(BF16) for i in range(8)]

    st.sems = {}
    for k in ("pe", "dve", "act", "sp", "pl", "sp_dma", "pl_dma", "act_dma"):
        st.sems[k] = e(nc.semaphore(f"sem_{k}"))
    st.block = e(nc.Block())
    return st


SL_A2, SL_E7, SL_HB4, SL_HB3, SL_THG, SL_EPS, SL_CB2 = 0, 1, 2, 3, 4, 5, 8


def program(ctx):
    st = ctx.S
    setup(ctx)
    for b in range(st.n_b):
        for q in range(NQ):
            phase_a(ctx, b, q)
    if st.pending_ln is not None:
        pb_, pq_, prse = st.pending_ln
        st.pending_ln = None
        ln_finish_a(ctx, pb_, pq_, prse)
    vwA = c_views(st, 0)
    vwB = c_views(st, 1)
    emit_c_reload(ctx, 0, 0, vwA)
    emit_c_reload(ctx, 0, 1, vwB)
    st.c_obv_pre = {(0, 0): emit_c_obv_load(ctx, 0, 0, vwA),
                    (0, 1): emit_c_obv_load(ctx, 0, 1, vwB)}
    phase_b(ctx)
    for b in range(st.n_b):
        for qa in range(0, NQ, 2):
            phase_c_pair(ctx, b, qa)
    if ctx.cur == "sp":
        ctx.raw_wait("sp_dma")
    if ctx.cur == "pl":
        ctx.raw_wait("pl_dma")
    if ctx.cur == "act":
        ctx.raw_wait("act_dma")


def wdma(ctx, qkey, dst, src):
    kc = src.shape[0] // 128
    slow = src.shape[1] < 16
    ctx.op(qkey, lambda e: e.dma_start(
        out=dst.rearrange("p (kc m) -> p kc m", kc=kc),
        in_=src.rearrange("(kc p) m -> p kc m", p=128),
        allow_slow_non_contiguous=slow))


def setup(ctx):
    st = ctx.S
    st.loaded = set()
    st.load_cnt = {}
    st.roll_user = {}
    st.spill_last = {}
    st.pb_reader = {}
    st.c_rld = {}
    st.out_dma_last = {}
    st.c_pre = {}
    st.out_tr = [None, None]
    st.spill_q = {}
    st.c_obv_pre = {}
    st.pending_ln = None

    def cdma(fn):
        ctx.op("sp_dma", fn)
    cdma(lambda e: e.dma_start(out=st.identf_sb[:], in_=st.ident_f))
    cdma(lambda e: e.dma_start(out=st.identb_sb[:], in_=st.ident_b))
    cdma(lambda e: e.dma_start(out=st.onesrow_sb[:], in_=st.ones_row))
    cdma(lambda e: e.dma_start(out=st.onescol_sb[:], in_=st.ones_col))
    cdma(lambda e: e.dma_start(out=st.mask0_sb[:], in_=st.mask0))
    cdma(lambda e: e.dma_start(out=st.maskM_sb[:], in_=st.maskM))
    cdma(lambda e: e.dma_start(out=st.srow[0:1, 16:17], in_=st.ar.unsqueeze(0)))
    cdma(lambda e: e.dma_start(out=st.srow[0:1, 17:18], in_=st.cn_b4.unsqueeze(0)))
    cdma(lambda e: e.dma_start(out=st.srow[0:1, 18:19], in_=st.gc_b3.unsqueeze(0)))
    st.c_sp_consts = ctx.n["sp_dma"]
    ctx.q_serialize("sp_dma")
    for tc in tc_needed(0):
        cdma(lambda e, tc=tc: e.dma_start(
            out=wv(st.xtok, tc % 6, 1024),
            in_=st.x[0, 128 * tc:128 * tc + 128, :]))
        st.loaded.add((0, tc))
    for tc in tc_needed(0):
        st.load_cnt[(0, tc)] = ("sp_dma", ctx.n["sp_dma"])
    ctx.q_serialize("sp_dma")
    for dst, src in ((st.bproj_sb, st.bproj), (st.bcn1_sb, st.cn_b1),
                     (st.bcn2_sb, st.cn_b2), (st.bcn3_sb, st.cn_b3),
                     (st.bgc1_sb, st.gc_b1), (st.bgc2_sb, st.gc_b2)):
        cdma(lambda e, dst=dst, src=src: e.dma_start(
            out=dst[:], in_=src.rearrange("(c p) -> p c", p=128),
            allow_slow_non_contiguous=True))
    st.c_sp_bias = ctx.n["sp_dma"]
    wdma(ctx, "pl_dma", st.wqkv_sb[:], st.wqkv)
    st.c_pl_wqkv = ctx.n["pl_dma"]
    ctx.q_serialize("pl_dma")
    wdma(ctx, "pl_dma", st.wproj_sb[:], st.wproj)
    st.c_pl_wproj = ctx.n["pl_dma"]
    ctx.q_serialize("pl_dma")

    dep = [("sp_dma", st.c_sp_consts)]
    r = st.srow
    ctx.op("dve", lambda e: e.tensor_scalar(
        r[0:1, SL_A2:SL_A2 + 1], r[0:1, 16:17], 0.3, None, ALU.mult), dep)
    ctx.op("dve", lambda e: e.tensor_scalar(
        r[0:1, SL_E7:SL_E7 + 1], r[0:1, 16:17], 0.7, None, ALU.mult))
    ctx.op("dve", lambda e: e.tensor_scalar(
        r[0:1, SL_HB4:SL_HB4 + 1], r[0:1, 17:18], 0.5, None, ALU.mult))
    ctx.op("dve", lambda e: e.tensor_scalar(
        r[0:1, SL_HB3:SL_HB3 + 1], r[0:1, 18:19], 0.5, None, ALU.mult))

    ctx.op("dve", lambda e: e.memset(r[0:1, SL_EPS:SL_EPS + 1], EPS))
    for i in (6, 7):
        ctx.op("dve", lambda e, i=i: e.memset(st.P[i][:, 192:256], 0.0))
    if ctx.cur in ("pe", "dve", "act"):
        ctx.raw_wait("sp_dma", st.c_sp_consts)
    if ctx.cur == "act":
        ctx._wait("dve", ctx.n["dve"])


def tc_needed(q):
    lo = max(0, 4 * q - 1)
    hi = min(16, 4 * q + 5)
    return list(range(lo, hi))


def emit_x_loads(ctx, b, q, tcs, split=False):
    st = ctx.S
    todo = [tc for tc in tcs if (b, tc) not in st.loaded]
    if not todo:
        return
    qs = {tc: ("sp_dma" if split and i % 2 else "pl_dma")
          for i, tc in enumerate(todo)}
    for qk in set(qs.values()):
        ctx.q_serialize(qk)
    for tc in todo:
        ctx.op(qs[tc], lambda e, tc=tc: e.dma_start(
            out=wv(st.xtok, tc % 6, 1024),
            in_=st.x[b, 128 * tc:128 * tc + 128, :]),
            deps=[("pe", None)])
        st.loaded.add((b, tc))
    for tc in todo:
        st.load_cnt[(b, tc)] = (qs[tc], ctx.n[qs[tc]])


def phase_a(ctx, b, q):
    st = ctx.S
    base = 512 * q - 8

    for tc in sorted(tc_needed(q), key=lambda tc: st.load_cnt[(b, tc)][1]):
        col0 = 128 * tc - base
        slot = tc % 6
        for f in range(8):
            bank = (f + 4 * tc) % 4
            deps = [st.load_cnt[(b, tc)]] if f == 0 else []
            ctx.mm(bank, lambda e, slot=slot, f=f, bank=bank: e.transpose(
                st.P[bank][:, 0:128],
                wv(st.xtok, slot, 1024, (128 * f, 128 * f + 128)),
                st.identf_sb[:]), deps)
            if col0 == -120:
                src = st.P[bank][:, 120:128]
                dst = wv(st.xtb, f, 528, (0, 8))
            elif col0 == 520:
                src = st.P[bank][:, 0:8]
                dst = wv(st.xtb, f, 528, (520, 528))
            else:
                src = st.P[bank][:, 0:128]
                dst = wv(st.xtb, f, 528, (col0, col0 + 128))
            ctx.rd("act", bank, lambda e, src=src, dst=dst: e.activation(
                dst, src, AF.Copy))
    xtb3 = st.xtb[:].rearrange("p (c n) -> p c n", c=8)
    if q == 0:
        ctx.op("dve", lambda e: e.memset(xtb3[:, :, 0:8], 0.0))
    if q == NQ - 1:
        ctx.op("dve", lambda e: e.memset(xtb3[:, :, 520:528], 0.0))
    dve_pads = ctx.n["dve"]

    nb_, nq_ = (b, q + 1) if q + 1 < NQ else (b + 1, 0)
    own_slots_now = {(4 * q + i) % 6 for i in range(4)}
    if nb_ < st.n_b:
        safe = [tc for tc in tc_needed(nq_)
                if (nb_, tc) not in st.loaded and tc % 6 not in own_slots_now]
        emit_x_loads(ctx, nb_, nq_, safe)

    if st.pending_ln is not None:
        pb_, pq_, prse = st.pending_ln
        st.pending_ln = None
        ln_finish_a(ctx, pb_, pq_, prse)

    xdep = [("act", None), ("dve", dve_pads)]
    for m in range(24):
        bank = 2 + m % 2
        tb = 4 + m // 12
        tcol = 16 * (m % 12)
        for k in range(8):
            lhs = wv(st.wqkv_sb, k, 3072, (128 * m, 128 * m + 128))
            deps = xdep if (k == 0 and m < 2) else []
            if m == 0 and k == 0:
                deps = deps + [("pl_dma", st.c_pl_wqkv)]
            ctx.mm(bank, lambda e, lhs=lhs, k=k, bank=bank: e.matmul(
                st.P[bank][:, 0:512], lhs, wv(st.xtb, k, 528, (0, 512)),
                start=k == 0, stop=k == 7), deps)
            ctx.mm(tb, lambda e, lhs=lhs, k=k, tb=tb, tcol=tcol: e.matmul(
                st.P[tb][:, tcol:tcol + 16], lhs, wv(st.xtb, k, 528, (512, 528)),
                start=k == 0, stop=k == 7))
        scale = 0.125 if m < 8 else 1.0
        ctx.rd("act", bank, lambda e, m=m, bank=bank, scale=scale: e.activation(
            wv(st.qkvb, m, 528, (0, 512)), st.P[bank][:, 0:512], AF.Copy,
            scale=scale))
    qkv3 = st.qkvb[:].rearrange("p (c n) -> p c n", c=24)
    ctx.rd("act", 4, lambda e: e.activation(
        qkv3[:, 0:8, 512:528],
        st.P[4][:, 0:128].rearrange("p (c n) -> p c n", c=8), AF.Copy, scale=0.125))
    ctx.rd("act", 4, lambda e: e.activation(
        qkv3[:, 8:12, 512:528],
        st.P[4][:, 128:192].rearrange("p (c n) -> p c n", c=4), AF.Copy))
    ctx.rd("act", 5, lambda e: e.activation(
        qkv3[:, 12:24, 512:528],
        st.P[5][:, 0:192].rearrange("p (c n) -> p c n", c=12), AF.Copy))
    st.act_qkv = ctx.n["act"]

    attention_quarter(ctx, q)

    f2v = st.f2[:].rearrange("p (c n) -> p c n", c=8)
    f1v = st.f1[:].rearrange("p (c n) -> p c n", c=8)
    proj_cnt = {}
    own = [(4 * q + i) % 6 for i in range(4)]
    otdep = [("act", None), ("dve", None)]
    for m in range(8):
        xb = 4 + m % 2
        for j, slot in enumerate(own):
            ctx.mm(xb, lambda e, slot=slot, m=m, j=j, xb=xb: e.transpose(
                st.P[xb][:, 128 * j:128 * j + 128],
                wv(st.xtok, slot, 1024, (128 * m, 128 * m + 128)),
                st.identf_sb[:]))
        ctx.rd("act", xb, lambda e, m=m, xb=xb: e.activation(
            f1v[:, m % 2, :], st.P[xb][:, 0:512], AF.Copy),
            [("dve", st.roll_user.get(m % 2, 0))])
        pb_ = 2 + m % 2
        for k in range(8):
            lhs = wv(st.wproj_sb, k, 1024, (128 * m, 128 * m + 128))
            ctx.mm(pb_, lambda e, lhs=lhs, k=k, pb_=pb_: e.matmul(
                st.P[pb_][:, 0:512], lhs, wv(st.otsb, k, 512),
                start=k == 0, stop=k == 7),
                otdep + [("pl_dma", st.c_pl_wproj)] if (k == 0 and m == 0) else [])
        st.roll_user[m % 2] = ctx.rd("dve", pb_, lambda e, m=m, pb_=pb_:
            e.scalar_tensor_tensor(
                f2v[:, m, :], st.P[pb_][:, 0:512], st.bproj_sb[:, m:m + 1],
                f1v[:, m % 2, :], ALU.add, ALU.add),
            [("act", None)] + ([("sp_dma", st.c_sp_bias)] if m == 0 else []))
        proj_cnt[m] = st.roll_user[m % 2]

    if nb_ < st.n_b:
        emit_x_loads(ctx, nb_, nq_, tc_needed(nq_), split=True)

    rse = ln_stats_rows_a(ctx, proj_cnt)
    st.pending_ln = (b, q, rse)

    if b == 0 and q == 1:
        ctx.q_serialize("pl_dma")
        wdma(ctx, "pl_dma", st.wcn1_sb[:], st.cn_w1)
        wdma(ctx, "pl_dma", st.wcn2_sb[:], st.cn_w2)
        wdma(ctx, "pl_dma", st.wcn3_sb[:], st.cn_w3)
        wdma(ctx, "pl_dma", st.wcn4_sb[:], st.cn_w4)
        wdma(ctx, "pl_dma", st.wgc1_sb[:], st.gc_w1)
        wdma(ctx, "pl_dma", st.wgc2_sb[:], st.gc_w2)
        wdma(ctx, "pl_dma", st.wgc3_sb[:], st.gc_w3)
        st.c_pl_wall = ctx.n["pl_dma"]


def attention_quarter(ctx, q):
    st = ctx.S
    chp = []
    for c in range(4):
        C = 4 * q + c
        wlo = max(0, 128 * C - 8)
        whi = min(SEQ, 128 * C + 136)
        chp.append((whi - wlo, wlo - (512 * q - 8), 128 * c + 8,
                    st.mask0_sb if C == 0 else st.maskM_sb, c))
    qkvdep = [("act", st.act_qkv)]
    pb_reader = ctx.S.pb_reader
    ts_cnt = {}
    vw_evac = {}
    exp_cnt = {}
    pt_evac = {}

    def s_front(h):
        W, wcol, qcol, mask, c = chp[h // 16]
        par = h % 2
        pp = h // 2
        sl6 = pp % 6
        bank = par + 4 * (sl6 // 3)
        co = 176 * (sl6 % 3)
        lhs = wv(st.qkvb, pp % 8, 528, (qcol, qcol + 128))[64 * par:64 * par + 64, :]
        rhs = wv(st.qkvb, 8 + pp % 8, 528, (wcol, wcol + W))[64 * par:64 * par + 64, :]
        ctx.mm(bank, lambda e, bank=bank: e.matmul(
            st.P[bank][:, co:co + W], lhs, rhs, start=True, stop=True))

    def stage1(pp):
        W, wcol, qcol, mask, c = chp[pp // 8]
        sl6 = pp % 6
        co = 176 * (sl6 % 3)
        for par in (0, 1):
            bank = par + 4 * (sl6 // 3)
            sl = 2 * (pp % 4) + par
            exp_cnt[pp] = ctx.rd("act", bank, lambda e, bank=bank, sl=sl: e.activation(
                st.pb[sl][:, 0:W], st.P[bank][:, co:co + W], AF.Exp),
                [("pe", pb_reader.get(sl, 0))])
            ctx.op("dve", lambda e, sl=sl, mask=mask, W=W: e.scalar_tensor_tensor(
                st.pmb[sl][:, 0:W], st.pb[sl][:, 0:W],
                1.0, mask[:, 0:W], ALU.mult, ALU.mult,
                accum_out=st.sacc[:, (sl % 2) + 2 * (pp % 2):(sl % 2) + 2 * (pp % 2) + 1]),
                [("act", exp_cnt[pp])])
        s0 = 2 * (pp % 2)
        ctx.op("dve", lambda e, s0=s0: e.reciprocal(
            st.sacc[:, 4 + s0:6 + s0], st.sacc[:, s0:s0 + 2]), [("dve", None)])
        for par in (0, 1):
            sl = 2 * (pp % 4) + par
            ts_cnt[pp] = ctx.op("dve", lambda e, sl=sl, s0=s0, par=par, W=W: e.tensor_scalar(
                st.pb[sl][:, 0:W], st.pmb[sl][:, 0:W],
                st.sacc[:, 4 + s0 + par:5 + s0 + par], None, ALU.mult),
                [("dve", None)])

    def stage2a(pp):
        W, wcol, qcol, mask, c = chp[pp // 8]
        vslot = pp % 4
        vbank = 6 + pp % 2
        vsrc = wv(st.qkvb, 16 + pp % 8, 528)
        ctx.mm(vbank, lambda e, vbank=vbank, vsrc=vsrc, wcol=wcol: e.transpose(
            st.Pb[vbank][:, 256:384], vsrc[:, wcol:wcol + 128], st.identb_sb[:]),
            qkvdep)
        ctx.mm(vbank, lambda e, vbank=vbank, vsrc=vsrc, wcol=wcol, W=W: e.transpose(
            st.Pb[vbank][0:W - 128, 384:512], vsrc[:, wcol + 128:wcol + W],
            st.identb_sb[:]))
        vw_evac[pp] = ctx.rd("act", vbank, lambda e, vbank=vbank, vslot=vslot: e.activation(
            wv(st.vwin, vslot, 256), st.Pb[vbank][:, 256:512], AF.Copy))
        for par in (0, 1):
            sl = 2 * (pp % 4) + par
            ptb = st.Pb[2 + par]
            ctx.mm(2 + par, lambda e, ptb=ptb, sl=sl: e.transpose(
                ptb[:, 0:128], st.pb[sl][:, 0:128], st.identb_sb[:]),
                [("dve", ts_cnt[pp])] if par == 0 else [])
            ctx.mm(2 + par, lambda e, ptb=ptb, sl=sl, W=W: e.transpose(
                ptb[0:W - 128, 128:256], st.pb[sl][:, 128:W], st.identb_sb[:]))
            pb_reader[sl] = ctx.n["pe"]
        for par in (0, 1):
            pt_evac[pp] = ctx.rd("dve", 2 + par, lambda e, par=par, pp=pp:
                e.tensor_copy(wv(st.ptsb, 2 * (pp % 3) + par, 256),
                              st.Pb[2 + par][:, 0:256]))

    def stage2b(pp):
        W, wcol, qcol, mask, c = chp[pp // 8]
        vslot = pp % 4
        for par in (0, 1):
            ob = 6 + par
            va = wv(st.vwin, vslot, 256, (64 * par, 64 * par + 64))
            vb = wv(st.vwin, vslot, 256, (128 + 64 * par, 128 + 64 * par + 64))
            ctx.mm(ob, lambda e, par=par, va=va, ob=ob, pp=pp: e.matmul(
                st.P[ob][0:64, 0:128], va,
                wv(st.ptsb, 2 * (pp % 3) + par, 256, (0, 128)),
                start=True, stop=False),
                [("dve", pt_evac[pp]), ("act", vw_evac[pp])] if par == 0 else [])
            ctx.mm(ob, lambda e, par=par, vb=vb, ob=ob, W=W, pp=pp: e.matmul(
                st.P[ob][0:64, 0:128], vb[0:W - 128, :],
                wv(st.ptsb, 2 * (pp % 3) + par, 256, (128, 256))[0:W - 128, :],
                start=False, stop=True))
        for par in (0, 1):
            ob = 6 + par
            ctx.rd("act", ob, lambda e, par=par, ob=ob, pp=pp, c=c: e.activation(
                wv(st.otsb, pp % 8, 512, (128 * c, 128 * c + 128))[64 * par:64 * par + 64, :],
                st.P[ob][0:64, 0:128], AF.Copy))

    D1, D2A, D2B = 3, 6, 9
    for step in range(32 + D2B):
        if step >= D2B:
            stage2b(step - D2B)
        if D2A <= step < 32 + D2A:
            stage2a(step - D2A)
        if D1 <= step < 32 + D1:
            stage1(step - D1)
        if step < 32:
            s_front(2 * step)
            s_front(2 * step + 1)


def ln_stats(ctx, src, src_cnt, banks, sq):
    st = ctx.S
    b1, b2 = banks
    sqc = {}
    mm2c = {}
    for m in range(8):
        sdep = [("dve", src_cnt[m] if src_cnt else None)]
        sqc[m] = ctx.op("act", lambda e, m=m: e.activation(
            sq[m % 2], src[:, m, :], AF.Square),
            sdep + ([("pe", None)] if m < 2 else [("pe", mm2c[m - 2])]))
        ctx.mm(b1, lambda e, m=m: e.matmul(
            st.P[b1][0:1, 0:512], st.onescol_sb[:], src[:, m, :],
            start=m == 0, stop=m == 7), sdep)
        mm2c[m] = ctx.mm(b2, lambda e, m=m: e.matmul(
            st.P[b2][0:1, 0:512], st.onescol_sb[:], sq[m % 2],
            start=m == 0, stop=m == 7), [("act", sqc[m])])
    return mm2c


def ln_rows(ctx, rows, t0, banks, mm2c7=None):
    st = ctx.S
    b1, b2 = banks
    mrow = rows[0:1, 0:512]
    rrow = rows[0:1, 512:1024]
    ctx.rd("dve", b1, lambda e: e.tensor_scalar(mrow, st.P[b1][0:1, 0:512],
                                                1.0 / D, None, ALU.mult))
    ctx.rd("dve", b2, lambda e: e.tensor_scalar(t0, st.P[b2][0:1, 0:512],
                                                1.0 / D, None, ALU.mult),
           [("pe", mm2c7)] if mm2c7 else [])
    ctx.op("dve", lambda e: e.tensor_tensor(rrow, mrow, mrow, ALU.mult),
           [("dve", None)])
    ctx.op("dve", lambda e: e.tensor_tensor(t0, t0, rrow, ALU.subtract),
           [("dve", None)])
    lnc = ctx.op("act", lambda e: e.activation(rrow, t0, AF.Ln,
        bias=st.srow[0:1, SL_EPS:SL_EPS + 1]), [("dve", None)])
    rse = ctx.op("act", lambda e: e.activation(rrow, rrow, AF.Exp, scale=-0.5),
                 [("act", lnc)])
    return rse


def ln_bcast(ctx, rows, banks, rse):
    st = ctx.S
    b1, b2 = banks
    ctx.mm(b1, lambda e: e.matmul(st.P[b1][:, 0:512], st.onesrow_sb[:],
                                  rows[0:1, 0:512], start=True, stop=True),
           [("dve", None)])
    ctx.mm(b2, lambda e: e.matmul(st.P[b2][:, 0:512], st.onesrow_sb[:],
                                  rows[0:1, 512:1024], start=True, stop=True),
           [("act", rse)])


def ln_norm(ctx, src, dst, tsc, banks, rse):
    st = ctx.S
    b1, b2 = banks
    norm_cnt = {}
    for m in range(8):
        ctx.rd("dve", b1, lambda e, m=m: e.tensor_tensor(
            tsc[:], src[:, m, :], st.P[b1][:, 0:512], ALU.subtract),
            [("act", rse), ("dve", None)] if m == 0 else [("dve", None)])
        norm_cnt[m] = ctx.rd("dve", b2, lambda e, m=m: e.scalar_tensor_tensor(
            dst[:, m, :], tsc[:], 1.0, st.P[b2][:, 0:512],
            ALU.mult, ALU.mult), [("dve", None)])
    return norm_cnt


def ln_stats_rows_a(ctx, proj_cnt):
    st = ctx.S
    src = st.f2[:].rearrange("p (c n) -> p c n", c=8)
    mm2c = ln_stats(ctx, src, proj_cnt, (6, 7), st.sq_a)
    return ln_rows(ctx, st.rows_a, st.tsc_a[0:1, 0:512], (6, 7), mm2c[7])


def ln_finish_a(ctx, b, q, rse):
    st = ctx.S
    b1, b2 = 6, 7
    src = st.f2[:].rearrange("p (c n) -> p c n", c=8)
    tsc = st.tsc_a
    mrow = st.rows_a[0:1, 0:512]
    rrow = st.rows_a[0:1, 512:1024]
    ctx.mm(b1, lambda e: e.matmul(st.P[b1][:, 0:512], st.onesrow_sb[:], mrow,
                                  start=True, stop=True))
    ctx.mm(b2, lambda e: e.matmul(st.P[b2][:, 0:512], st.onesrow_sb[:], rrow,
                                  start=True, stop=True), [("act", rse)])
    x1v = st.x1roll.rearrange("p (c n) -> p c n", c=2)
    spill_cnt = {}
    for m in range(8):
        ctx.rd("dve", b1, lambda e, m=m: e.tensor_tensor(
            tsc[:], src[:, m, :], st.P[b1][:, 0:512], ALU.subtract),
            [("dve", None)])
        acc = wv(st.gcparts, m, 16, (4 * b + q, 4 * b + q + 1))
        deps = ([("sp_dma", spill_cnt[m - 2])] if m >= 2
                else ([("sp_dma", st.spill_last[m % 2])]
                      if m % 2 in st.spill_last else []))
        ctx.rd("dve", b2, lambda e, m=m, acc=acc: e.scalar_tensor_tensor(
            x1v[:, m % 2, :], tsc[:], 1.0, st.P[b2][:, 0:512],
            ALU.mult, ALU.mult, accum_out=acc), deps + [("dve", None)])
        ctx.q_serialize("sp_dma")
        spill_cnt[m] = ctx.op("sp_dma", lambda e, m=m: e.dma_start(
            out=st.x1sp[b, m, :, 512 * q:512 * q + 512], in_=x1v[:, m % 2, :]),
            deps=[("dve", None)])
        st.spill_last[m % 2] = spill_cnt[m]
    st.spill_q[(b, q)] = spill_cnt[7]


def phase_b(ctx):
    st = ctx.S
    gp = st.gcparts[:].rearrange("p (c n) -> p c n", c=8)
    gs = st.gcsum[:].rearrange("p (b n) -> p b n", b=4)
    gb = st.gcbf[:].rearrange("p (b n) -> p b n", b=4)
    for b in range(st.n_b):
        dep = [("dve", None), ("sp_dma", st.c_sp_bias)] if b == 0 else []
        ctx.op("dve", lambda e, b=b: e.tensor_tensor(
            gs[:, b, :], gp[:, :, 4 * b + 0], gp[:, :, 4 * b + 1], ALU.add),
            dep)
        ctx.op("dve", lambda e, b=b: e.tensor_tensor(
            gs[:, b, :], gs[:, b, :], gp[:, :, 4 * b + 2], ALU.add),
            [("dve", None)])
        ctx.op("dve", lambda e, b=b: e.tensor_tensor(
            gs[:, b, :], gs[:, b, :], gp[:, :, 4 * b + 3], ALU.add),
            [("dve", None)])
        ctx.op("dve", lambda e, b=b: e.tensor_scalar(
            gs[:, b, :], gs[:, b, :], 1.0 / SEQ, None, ALU.mult),
            [("dve", None)])
        ctx.op("dve", lambda e, b=b: e.tensor_copy(gb[:, b, :], gs[:, b, :]),
               [("dve", None)])
    for m in range(4):
        for b in range(st.n_b):
            for k in range(8):
                lhs = wv(st.wgc1_sb, k, 512, (128 * m, 128 * m + 128))
                ctx.mm(b, lambda e, lhs=lhs, k=k, m=m, b=b: e.matmul(
                    st.P[b][:, m:m + 1], lhs, gb[:, b, k:k + 1],
                    start=k == 0, stop=k == 7),
                    [("dve", None), ("pl_dma", st.c_pl_wall)]
                    if (m == 0 and k == 0) else [])
        for b in range(st.n_b):
            ctx.rd("act", b, lambda e, m=m, b=b: e.activation(
                st.g1bf[:, 4 * b + m:4 * b + m + 1], st.P[b][:, m:m + 1],
                AF.Gelu, bias=st.bgc1_sb[:, m:m + 1]))
    for m in range(2):
        for b in range(st.n_b):
            for k in range(4):
                lhs = wv(st.wgc2_sb, k, 256, (128 * m, 128 * m + 128))
                ctx.mm(b, lambda e, lhs=lhs, k=k, m=m, b=b: e.matmul(
                    st.P[b][:, 128 + m:129 + m], lhs,
                    st.g1bf[:, 4 * b + k:4 * b + k + 1],
                    start=k == 0, stop=k == 3),
                    [("act", None)] if (m == 0 and b == 0) else [])
        for b in range(st.n_b):
            ctx.rd("act", b, lambda e, m=m, b=b: e.activation(
                st.g2bf[:, 2 * b + m:2 * b + m + 1],
                st.P[b][:, 128 + m:129 + m],
                AF.Gelu, bias=st.bgc2_sb[:, m:m + 1]))
    for b in range(st.n_b):
        for k in range(2):
            ctx.mm(b, lambda e, k=k, b=b: e.matmul(
                st.P[b][0:1, 256:257], st.wgc3_sb[:, k:k + 1],
                st.g2bf[:, 2 * b + k:2 * b + k + 1],
                start=k == 0, stop=k == 1),
                [("act", None)] if (k == 0 and b == 0) else [])
    for b in range(st.n_b):
        ctx.rd("act", b, lambda e, b=b: e.activation(
            st.srow[0:1, 20 + b:21 + b], st.P[b][0:1, 256:257], AF.Tanh,
            scale=0.5, bias=st.srow[0:1, SL_HB3:SL_HB3 + 1]))
    for b in range(st.n_b):
        ctx.op("dve", lambda e, b=b: e.tensor_scalar(
            st.srow[0:1, SL_CB2 + b:SL_CB2 + b + 1], st.srow[0:1, 20 + b:21 + b],
            st.srow[0:1, SL_E7:SL_E7 + 1], 0.1, ALU.mult, ALU.add),
            [("act", None)] + ([("dve", None)] if b == 0 else []))


TW_SRC = [4, 5, 6, 7, 0, 1, 2, 3]
TW_SGN = [1.0, 1.0, -1.0, -1.0, -1.0, -1.0, 1.0, 1.0]


def c_views(st, side):
    if side == 0:
        return dict(
            x1=st.xtok[:, 2048:6144].rearrange("p (c n) -> p c n", c=8),
            s0=st.f1[:].rearrange("p (c n) -> p c n", c=8),
            s1=st.f2[:].rearrange("p (c n) -> p c n", c=8),
            obv=st.qkvb[:, 7168:11264].rearrange("p (c n) -> p c n", c=8),
            tsc=st.tsc_c, rows=st.rows_c)
    return dict(
        x1=st.x1B.rearrange("p (c n) -> p c n", c=8),
        s0=st.sb0.rearrange("p (c n) -> p c n", c=8),
        s1=st.sb1.rearrange("p (c n) -> p c n", c=8),
        obv=st.obvB.rearrange("p (c n) -> p c n", c=8),
        tsc=st.tscB, rows=st.rowsB)


def emit_c_reload(ctx, b, q, vw_):
    st = ctx.S
    ctx.q_serialize("sp_dma")
    ctx.op("sp_dma", lambda e: e.dma_start(
        out=vw_["x1"],
        in_=st.x1sp[b, :, :, 512 * q:512 * q + 512].rearrange(
            "m p t -> p m t")),
        deps=[("dve", None), ("pe", None), ("act", None), ("pl", None)])
    st.c_rld[(b, q)] = ctx.n["sp_dma"]


def emit_c_obv_load(ctx, b, q, vw_):
    st = ctx.S
    ctx.q_serialize("pl_dma")
    ctx.op("pl_dma", lambda e: e.dma_start(
        out=vw_["obv"],
        in_=st.x1sp[b, :, :, 512 * q:512 * q + 512].rearrange(
            "m p t -> p m t")),
        deps=[("pe", None), ("act", None), ("sp_dma", st.spill_q[(b, q)])])
    cnt = ctx.n["pl_dma"]
    return {m: ("pl_dma", cnt) for m in range(8)}


def c_chain(ctx, b, vw_, obv_cnt, cb):
    st = ctx.S
    h1v = st.qkvb[:, 0:4096].rearrange("p (c n) -> p c n", c=8)
    h2v = st.qkvb[:, 4096:6144].rearrange("p (c n) -> p c n", c=4)
    h3v = st.qkvb[:, 6144:7168].rearrange("p (c n) -> p c n", c=2)
    obv = vw_["obv"]
    throw_ = vw_["rows"][0:1, 0:512]
    crow = vw_["rows"][0:1, 512:1024]
    h1e = {}
    for half in range(2):
        for k in range(8):
            for mi in range(4):
                m = 4 * half + mi
                ctx.mm(mi, lambda e, m=m, k=k, mi=mi: e.matmul(
                    st.P[mi][:, 0:512],
                    wv(st.wcn1_sb, k, 1024, (128 * m, 128 * m + 128))
                    if False else st.wcn1v[k][:, 128 * m:128 * m + 128],
                    obv[:, k, :], start=k == 0, stop=k == 7),
                    [obv_cnt[k]] if mi == 0 and half == 0 else [])
        for mi in range(4):
            m = 4 * half + mi
            h1e[m] = ctx.rd("act", mi, lambda e, m=m, mi=mi: e.activation(
                h1v[:, m, :], st.P[mi][:, 0:512], AF.Gelu,
                bias=st.bcn1_sb[:, m:m + 1]))
    for k in range(8):
        for m in range(4):
            ctx.mm(m, lambda e, m=m, k=k: e.matmul(
                st.P[m][:, 0:512], st.wcn2v[k][:, 128 * m:128 * m + 128],
                h1v[:, k, :], start=k == 0, stop=k == 7),
                [("act", h1e[k])] if m == 0 else [])
    h2e = {}
    for m in range(4):
        h2e[m] = ctx.rd("act", m, lambda e, m=m: e.activation(
            h2v[:, m, :], st.P[m][:, 0:512], AF.Gelu, bias=st.bcn2_sb[:, m:m + 1]))
    for k in range(4):
        for m in range(2):
            ctx.mm(m, lambda e, m=m, k=k: e.matmul(
                st.P[m][:, 0:512], st.wcn3v[k][:, 128 * m:128 * m + 128],
                h2v[:, k, :], start=k == 0, stop=k == 3),
                [("act", h2e[k])] if m == 0 else [])
    h3e = {}
    for m in range(2):
        h3e[m] = ctx.rd("act", m, lambda e, m=m: e.activation(
            h3v[:, m, :], st.P[m][:, 0:512], AF.Gelu, bias=st.bcn3_sb[:, m:m + 1]))
    for k in range(2):
        ctx.mm(4, lambda e, k=k: e.matmul(
            st.P[4][0:1, 0:512], st.wcn4_sb[:, k:k + 1], h3v[:, k, :],
            start=k == 0, stop=k == 1), [("act", h3e[k])])
    ctx.rd("act", 4, lambda e: e.activation(
        throw_, st.P[4][0:1, 0:512], AF.Tanh, scale=0.5,
        bias=st.srow[0:1, SL_HB4:SL_HB4 + 1]))
    ctx.op("dve", lambda e: e.tensor_scalar(
        crow, throw_, st.srow[0:1, SL_A2:SL_A2 + 1],
        st.srow[0:1, SL_CB2 + b:SL_CB2 + b + 1], ALU.mult, ALU.add),
        [("act", None), ("dve", None)])
    ctx.mm(cb, lambda e: e.matmul(st.P[cb][:, 0:512], st.onesrow_sb[:], crow,
                                  start=True, stop=True), [("dve", None)])


def c_twist(ctx, vw_, cyc, curv, nxtv, obv_cnt, cb, extra_dep=()):
    st = ctx.S
    tsc = vw_["tsc"]
    for m in range(8):
        d = list(extra_dep) if m == 0 else []
        ctx.rd("dve", cb, lambda e, m=m: e.scalar_tensor_tensor(
            tsc[:], st.P[cb][:, 0:512], TW_SGN[m], curv[:, TW_SRC[m], :],
            ALU.mult, ALU.mult), d + [("dve", None)])
        adv = ctx.op("dve", lambda e, m=m: e.tensor_tensor(
            nxtv[:, m, :], curv[:, m, :], tsc[:], ALU.add), [("dve", None)])
        if cyc < 2:
            obv_cnt[m] = ("pl", ctx.op("pl", lambda e, m=m: e.tensor_copy(
                vw_["obv"][:, m, :], nxtv[:, m, :]), [("dve", adv)]))


def c_out(ctx, b, q, srcv, norm_cnt, side):
    st = ctx.S
    ctx.q_serialize("act_dma")
    for m in range(8):
        ctx.op("act_dma", lambda e, m=m: e.dma_start(
            out=st.out[b, m, :, 512 * q:512 * q + 512], in_=srcv[:, m, :]),
            deps=[("dve", norm_cnt[m])])
    st.out_tr[side] = ("act_dma", ctx.n["act_dma"])



def phase_c_pair(ctx, b, qa):
    st = ctx.S
    vwA = c_views(st, 0)
    vwB = c_views(st, 1)
    first_dep = [("pe", None), ("dve", None), ("act", None)]

    cA = st.c_pre.pop((0, b, qa), None)
    if cA is None:
        if (b, qa) not in st.c_rld:
            emit_c_reload(ctx, b, qa, vwA)
        cA = st.c_obv_pre.pop((b, qa), None) or             emit_c_obv_load(ctx, b, qa, vwA)
        c_chain(ctx, b, vwA, cA, 5)
    cB = st.c_pre.pop((1, b, qa + 1), None)
    if cB is None:
        if (b, qa + 1) not in st.c_rld:
            emit_c_reload(ctx, b, qa + 1, vwB)
        cB = st.c_obv_pre.pop((b, qa + 1), None) or             emit_c_obv_load(ctx, b, qa + 1, vwB)
        c_chain(ctx, b, vwB, cB, 7)
    nb_, nqa = (b, qa + 2) if qa + 2 < NQ else (b + 1, 0)
    have_next = nb_ < st.n_b
    otlA = ([st.out_tr[0]] if st.out_tr[0] else [("pe", None)]) + \
        [("sp_dma", st.c_rld[(b, qa)])]
    otlB = ([st.out_tr[1]] if st.out_tr[1] else [("pe", None)]) + \
        [("sp_dma", st.c_rld[(b, qa + 1)])]
    c_twist(ctx, vwA, 0, vwA["x1"], vwA["s0"], cA, 5, extra_dep=otlA)
    c_twist(ctx, vwB, 0, vwB["x1"], vwB["s0"], cB, 7, extra_dep=otlB)
    c_chain(ctx, b, vwA, cA, 5)
    c_twist(ctx, vwA, 1, vwA["s0"], vwA["s1"], cA, 5)
    c_chain(ctx, b, vwB, cB, 7)
    c_twist(ctx, vwB, 1, vwB["s0"], vwB["s1"], cB, 7)
    c_chain(ctx, b, vwA, cA, 5)
    c_twist(ctx, vwA, 2, vwA["s1"], vwA["s0"], cA, 5)
    if have_next:
        cNA = emit_c_obv_load(ctx, nb_, nqa, vwA)
    c_chain(ctx, b, vwB, cB, 7)
    c_twist(ctx, vwB, 2, vwB["s1"], vwB["s0"], cB, 7)
    if have_next:
        cNB = emit_c_obv_load(ctx, nb_, nqa + 1, vwB)
    resA = {}
    resB = {}
    for m in range(8):
        resA[m] = ctx.op("dve", lambda e, m=m: e.tensor_tensor(
            vwA["s1"][:, m, :], vwA["x1"][:, m, :], vwA["s0"][:, m, :], ALU.add),
            [("dve", None)] if m == 0 else [])
    for m in range(8):
        resB[m] = ctx.op("dve", lambda e, m=m: e.tensor_tensor(
            vwB["s1"][:, m, :], vwB["x1"][:, m, :], vwB["s0"][:, m, :], ALU.add),
            [("dve", None)] if m == 0 else [])
    if have_next:
        emit_c_reload(ctx, nb_, nqa, vwA)
        emit_c_reload(ctx, nb_, nqa + 1, vwB)
        c_chain(ctx, nb_, vwA, cNA, 5)
        c_chain(ctx, nb_, vwB, cNB, 7)
        st.c_pre[(0, nb_, nqa)] = cNA
        st.c_pre[(1, nb_, nqa + 1)] = cNB
    ln_stats(ctx, vwA["s1"], resA, (0, 1), st.sq_cA)
    ln_stats(ctx, vwB["s1"], resB, (2, 3), st.sq_cB)
    rseA = ln_rows(ctx, vwA["rows"], vwA["tsc"][0:1, 0:512], (0, 1))
    rseB = ln_rows(ctx, vwB["rows"], vwB["tsc"][0:1, 0:512], (2, 3))
    ln_bcast(ctx, vwA["rows"], (0, 1), rseA)
    ln_bcast(ctx, vwB["rows"], (2, 3), rseB)
    nA = ln_norm(ctx, vwA["s1"], vwA["s0"], vwA["tsc"], (0, 1), rseA)
    c_out(ctx, b, qa, vwA["s0"], nA, 0)
    nB = ln_norm(ctx, vwB["s1"], vwB["s0"], vwB["tsc"], (2, 3), rseB)
    c_out(ctx, b, qa + 1, vwB["s0"], nB, 1)


def build_nc(n_b=NB):
    nc = bass.Bass("TRN2", target_bir_lowering=False)
    st = build_state(nc, n_b)
    blk = st.block
    for name, attr in (("sp", "sync"), ("pl", "gpsimd"), ("dve", "vector"),
                       ("act", "scalar"), ("pe", "tensor")):
        def make(name):
            def run(eng):
                program(Ctx(st, name, eng))
            return run
        getattr(blk, attr)(make(name))
    st.ex.close()
    return nc


def kernel(L0_attn_x, L0_attn_Wqkv, L0_attn_Wproj, L0_attn_bproj,
           L0_ffn_cn_w1, L0_ffn_cn_b1, L0_ffn_cn_w2, L0_ffn_cn_b2,
           L0_ffn_cn_w3, L0_ffn_cn_b3, L0_ffn_cn_w4, L0_ffn_cn_b4,
           L0_ffn_gc_w1, L0_ffn_gc_b1, L0_ffn_gc_w2, L0_ffn_gc_b2,
           L0_ffn_gc_w3, L0_ffn_gc_b3, L0_ffn_adaptive_range):
    x = np.ascontiguousarray(np.asarray(L0_attn_x, dtype=np.float32))
    shared = {
        "wqkv": np.ascontiguousarray(L0_attn_Wqkv, dtype=np.float32),
        "wproj": np.ascontiguousarray(L0_attn_Wproj, dtype=np.float32),
        "bproj": np.ascontiguousarray(L0_attn_bproj, dtype=np.float32),
        "cn_w1": np.ascontiguousarray(L0_ffn_cn_w1, dtype=np.float32),
        "cn_b1": np.ascontiguousarray(L0_ffn_cn_b1, dtype=np.float32),
        "cn_w2": np.ascontiguousarray(L0_ffn_cn_w2, dtype=np.float32),
        "cn_b2": np.ascontiguousarray(L0_ffn_cn_b2, dtype=np.float32),
        "cn_w3": np.ascontiguousarray(L0_ffn_cn_w3, dtype=np.float32),
        "cn_b3": np.ascontiguousarray(L0_ffn_cn_b3, dtype=np.float32),
        "cn_w4": np.ascontiguousarray(L0_ffn_cn_w4, dtype=np.float32),
        "cn_b4": np.ascontiguousarray(L0_ffn_cn_b4, dtype=np.float32),
        "gc_w1": np.ascontiguousarray(L0_ffn_gc_w1, dtype=np.float32),
        "gc_b1": np.ascontiguousarray(L0_ffn_gc_b1, dtype=np.float32),
        "gc_w2": np.ascontiguousarray(L0_ffn_gc_w2, dtype=np.float32),
        "gc_b2": np.ascontiguousarray(L0_ffn_gc_b2, dtype=np.float32),
        "gc_w3": np.ascontiguousarray(L0_ffn_gc_w3, dtype=np.float32),
        "gc_b3": np.ascontiguousarray(L0_ffn_gc_b3, dtype=np.float32),
        "ar": np.ascontiguousarray(L0_ffn_adaptive_range, dtype=np.float32),
        **_const_inputs(),
    }
    nc = build_nc(NB)
    in_maps = [dict(shared, x=np.ascontiguousarray(x[4 * c:4 * c + 4]))
               for c in range(N_CORE)]
    res = run_bass_kernel_spmd(nc, in_maps, core_ids=list(range(N_CORE)))
    arr = np.concatenate([r["out"] for r in res.results], axis=0)
    return np.ascontiguousarray(
        arr.transpose(0, 3, 1, 2).reshape(32 * N_CORE // 8, SEQ, D))
